# version 16
# baseline (speedup 1.0000x reference)
import math
import numpy as np
import concourse.bass as bass
import concourse.mybir as mybir
from concourse.bass_utils import run_bass_kernel_spmd
from contextlib import ExitStack

F32 = mybir.dt.float32
BF16 = mybir.dt.bfloat16
AF = mybir.ActivationFunctionType
ALU = mybir.AluOpType
AX = mybir.AxisListType

LNSC = -0.5 * math.log(128.0)
EPS = 1e-5
FEPS = 1e-10
NINV = 1.0 / 16.0


class GlaBuilder:
    def __init__(self, D=2048, T=4096, TT=512, RK=16, ncores=8):
        self.ncores = ncores
        DK = D // 2
        self.D, self.DK, self.T, self.TT, self.RK = D, DK, T, TT, RK
        self.NI = D // 128
        self.H = D // 128
        self.HP = self.H // 2
        self.NM = T // TT
        self.NCH = TT // 128
        self.SL = 512
        self.HB = max(1, self.NI // 4)
        self.NQ = self.NI // self.HB
        assert D % 512 == 0 and TT == 512 and T % TT == 0
        assert self.NM > self.D // self.SL
        self.nc = bass.Bass("TRN2", target_bir_lowering=False)
        self.ctx = ExitStack()
        self.prog = {k: [] for k in ("sync", "gpsimd", "vector", "scalar", "tensor")}
        self.sem = {}
        self.cnt = {}
        self._declare()

    def em(self, eng, fn):
        if eng in ("vector", "scalar"):
            so = "sVo" if eng == "vector" else "sSo"
            su = "sV" if eng == "vector" else "sS"
            self.wait(eng, so)
            self.wait(eng, su)
            entry = ["op", fn, so, 1]
            self.prog[eng].append(entry)
            self.cnt[so] += 1
        else:
            self.prog[eng].append(["op", fn, None, 0])

    def inc(self, eng, s, k=1):
        entry = self.prog[eng][-1]
        assert entry[0] == "op"
        if entry[2] is not None:
            self.cnt[entry[2]] -= entry[3]
        entry[2] = s
        entry[3] = k
        self.cnt[s] += k
        return self.cnt[s]

    def wait(self, eng, s, v=None):
        v = self.cnt[s] if v is None else v
        if v > 0 and self.lastw.setdefault(eng, {}).get(s, 0) < v:
            self.lastw[eng][s] = v
            h = self.sem[s]
            self.prog[eng].append(["w", h, v])

    def dma(self, q, out, in_, s, slow=False):
        h = self.sem[s]
        self.prog[q].append(["op", lambda e, out=out, in_=in_, slow=slow:
                             e.dma_start(out=out, in_=in_,
                                         allow_slow_non_contiguous=slow), s, 16])
        self.cnt[s] += 16
        return self.cnt[s]

    def _declare(self):
        nc, ctx = self.nc, self.ctx
        D, DK, T, TT, RK = self.D, self.DK, self.T, self.TT, self.RK
        NI, H, HP, NM, NCH, SL, HB = (self.NI, self.H, self.HP, self.NM, self.NCH,
                                      self.SL, self.HB)

        def din(name, shape):
            return nc.dram_tensor(name, shape, F32, kind="ExternalInput").ap()
        self.xT = din("xT", [D, T])
        self.WqT = din("WqT", [D, DK])
        self.WkT = din("WkT", [D, DK])
        self.Wkg1T = din("Wkg1T", [D, RK])
        self.Wkg2T = din("Wkg2T", [RK, DK])
        self.bkg2 = din("bkg2", [1, DK])
        self.WvT = din("WvT", [D, D])
        self.WgT = din("WgT", [D, D])
        self.bgi = din("bgi", [1, D])
        self.WoT = din("WoT", [D, D])
        self.WfT = din("WfT", [D, D])
        self.bfi = din("bfi", [1, D])
        self.flag = din("flag", [128, 1])
        self.selv = din("selv", [128, 8])
        self.outT = nc.dram_tensor("outT", [D, T], F32, kind="ExternalOutput").ap()
        self.ksc = nc.dram_tensor("ksc", [NM, 128, HP * TT], BF16, kind="Internal").ap()
        self.vsc = nc.dram_tensor("vsc", [NM, 128, NCH * D], BF16, kind="Internal").ap()
        self.wtsc = nc.dram_tensor("wtsc", [D // SL, 128, NI * SL], BF16,
                                   kind="Internal").ap()
        self.scrd = nc.dram_tensor("scrd", [NM + 1, TT], F32, kind="Internal").ap()
        nqt = 4 * (DK // SL if DK >= SL else 1) * 2 + 4 * (D // SL) * 3
        self.wsc = nc.dram_tensor("wsc", [nqt, 128, HB * SL], BF16, kind="Internal").ap()
        self.s_loc = nc.dram_tensor("s_loc", [128, HP * 128], F32, kind="Internal").ap()
        self.s_all = nc.dram_tensor("s_all", [self.ncores * 128, HP * 128], F32, kind="Internal").ap()
        self.scrb = nc.dram_tensor("scrb", [NM, 2, TT], BF16, kind="Internal").ap()

        def sb(name, shape, dt=F32):
            t = ctx.enter_context(nc.sbuf_tensor(name, shape, dt))
            setattr(self, name, t)
            return t
        sb("otm", [128, H * 128], BF16)
        self.slabs = [sb(f"slab{i}", [128, HB * SL], BF16) for i in range(3)] + [self.otm]
        sb("xb", [128, NI * TT], BF16)
        sb("xf", [128, NI * TT], F32)
        sb("k_sb", [128, HP * TT], BF16)
        sb("q_sb", [128, HP * TT], BF16)
        sb("vtm", [128, NCH * D], BF16)
        self.ao = self.vtm
        sb("gk", [128, NCH * DK], BF16)
        sb("sg", [128, NCH * D], BF16)
        sb("yfm", [128, NI * TT], BF16)
        sb("kg1s", [RK, TT], BF16)
        sb("ktm", [128, HP * 128], BF16)
        sb("qt_sb", [128, HP * 128], BF16)
        sb("ptm", [128, 1024], BF16)
        sb("sqacc", [128, TT], F32)
        sb("eB", [128, HP * 128], BF16)
        sb("emB", [128, HP * 128], BF16)
        sb("eA", [128, HP], F32)
        sb("S", [128, HP * 128], F32)
        sb("S16", [128, HP * 128], BF16)
        sb("bns", [128, H * 6], F32)
        sb("bagg", [128, 2 * H], F32)
        sb("hr", [128, H], F32)
        sb("sqfA", [128, TT], F32)
        sb("sqfB", [128, TT], F32)
        sb("yrbb", [128, TT], F32)
        sb("r1col", [128, NM * NCH], F32)
        sb("svs", [128, 4 * TT], F32)
        sb("svb", [128, 2 * TT], BF16)
        sb("r1b", [128, TT], F32)
        sb("r3b", [128, TT], F32)
        sb("ut", [128, 128], BF16)
        sb("trimask4", [128, 512], F32)
        sb("idn", [128, 128], BF16)
        sb("ones_c", [128, 1], BF16)
        sb("ones_r", [1, 128], BF16)
        sb("onesf_r", [1, 128], F32)
        sb("onesf_c", [128, 1], F32)
        sb("selc", [128, 8], F32)
        sb("wkg1", [128, NI * RK], BF16)
        sb("wkg2", [RK, DK], BF16)
        sb("b2_sb", [1, DK], BF16)
        sb("bg_sb", [1, D], BF16)
        sb("bf_sb", [1, D], BF16)
        self.wsum = {}
        for m, w in (("q", DK), ("k", DK), ("kg1", RK), ("v", D), ("g", D), ("t", D)):
            self.wsum[m] = ctx.enter_context(nc.sbuf_tensor(f"wsum_{m}", [1, w], BF16))
        sb("epsc", [128, 1], F32)
        sb("onefc", [128, 1], F32)
        sb("lnscc", [128, 1], F32)
        sb("lnS_p", [1, 1], F32)
        sb("lnS_n", [1, 1], F32)
        sb("shalf", [128, 1], F32)
        sb("nshalf", [128, 1], F32)
        self.prb = self.r3b
        self.pmb = self.r1b
        self.Sbr_ap = self.xf[:, 0:HP * 128]
        self.mu_r = self.svs[0:1, 0:TT]
        self.e2_r = self.svs[0:1, TT:2 * TT]
        self.rA_r = self.svs[0:1, 2 * TT:3 * TT]
        self.murA_r = self.svs[0:1, 3 * TT:4 * TT]
        self.fs_r = self.yrbb[0:1, 0:192]
        self.nmu1_r = self.svb[0:1, 0:TT]
        self.r1inv_r = self.svb[0:1, TT:2 * TT]
        self.nmu3_r = self.svb[0:1, 0:TT]
        self.r3is_r = self.svb[0:1, TT:2 * TT]

        def psd(name, shape, dt=F32):
            return ctx.enter_context(nc.psum_tensor(name, shape, dt))
        self.pA = psd("pA", [128, 512])
        self.pB = psd("pB", [128, 512])
        self.pb2 = psd("pb2", [128, 512])
        self.pb3 = psd("pb3", [128, 512])
        self.pb4 = psd("pb4", [128, 512])
        self.pb5 = psd("pb5", [128, 512])
        self.ptr = psd("ptr", [128, 1024], BF16)
        self.pst = psd("pst", [128, 512])

        for s in ("sWDg0", "sWDg1", "sWDg2", "sWDg3", "sWDs0", "sWDs1", "sWDs2", "sWDs3", "sCW", "sXF", "sKS", "sVS", "sXDg",
                  "sOUT", "sKW", "sVW", "sFW", "sMD", "sMP1", "sMP2", "sFLG",
                  "sPR", "sPL", "sT", "sV", "sS", "sG", "sR", "sL", "sVo", "sSo"):
            self.sem[s] = ctx.enter_context(nc.semaphore(s))
            self.cnt[s] = 0
        self.lastw = {}
        self._sqv = [0, 0]
        self.slab_free = [dict(sT=0, sV=0, sS=0, sCW=0) for _ in range(4)]
        self.slab_i = 0
        self.wq_idx = {}
        self.wq_n = 0

    def v3(self, t, w=None):
        w = self.TT if w is None else w
        return t[:].rearrange("p (i t) -> p i t", t=w)

    def slab_load(self, src_ap, cast, ocols=None, cache_key=None, m0=False):
        ocols = self.SL if ocols is None else ocols
        if cache_key is not None:
            qi = self.wq_idx.setdefault(cache_key, len(self.wq_idx))
            if not m0:
                src_ap = self.wsc[qi][:, 0:self.HB * ocols].rearrange(
                    "p (i t) -> p i t", t=ocols)
                cast = False
        slot = self.slab_i % 4
        fr = self.slab_free[slot]
        q = "gpsimd" if cast else "sync"
        sem = ("sWDg" if cast else "sWDs") + str(slot)
        for s, v in fr.items():
            if v:
                self.wait(q, s, v)
        dst = self.slabs[slot][:, 0:self.HB * ocols].rearrange("p (i t) -> p i t",
                                                               t=ocols)
        tgt = (sem, self.dma(q, dst, src_ap, sem))
        if cache_key is not None and m0:
            self.wait("sync", sem, tgt[1])
            self.dma("sync", self.wsc[qi][:, 0:self.HB * ocols],
                     self.slabs[slot][:, 0:self.HB * ocols], "sCW")
        self.slab_i += 1
        return self.slabs[slot], tgt, slot

    def slab_done(self, slot):
        self.slab_free[slot] = dict(sT=self.cnt["sT"], sV=0, sS=0, sCW=self.cnt["sCW"])

    def whalf_src(self, srcT, og, h, ocols):
        HB = self.HB
        return (srcT[HB * 128 * h:HB * 128 * (h + 1), og * ocols:(og + 1) * ocols]
                .rearrange("(i p) o -> p i o", p=128))

    def bcast_f32(self, src_row, dst_tile):
        TT = self.TT
        self.wait("tensor", "sV")
        self.em("tensor", lambda e, src_row=src_row: e.matmul(
            self.pA[:, 0:TT], self.onesf_r[:], src_row, start=True, stop=True))
        self.inc("tensor", "sT")
        self.wait("vector", "sT")
        self.em("vector", lambda e, dst_tile=dst_tile: e.tensor_copy(dst_tile,
                                                                     self.pA[:, 0:TT]))
        self.inc("vector", "sV")

    def pscatter(self, src_row, dst_col, W):
        self.wait("sync", "sMP2")
        self.dma("sync", self.scrd[self.NM, 0:W], src_row, "sMP1")
        self.wait("sync", "sMP1")
        self.dma("sync", dst_col,
                 self.scrd[self.NM, 0:W].rearrange("(a p) -> p a", p=128), "sMP2",
                 slow=True)

    def pgather(self, src_col, dst_row, P):
        self.wait("sync", "sMP2")
        self.dma("sync", self.scrd[self.NM, 0:P].rearrange("(p a) -> p a", a=1),
                 src_col, "sMP1", slow=True)
        self.wait("sync", "sMP1")
        self.dma("sync", dst_row,
                 self.scrd[self.NM, 0:P].rearrange("(a p) -> a p", a=1), "sMP2")

    def emit_setup(self):
        g = "gpsimd"
        consts = [(self.ones_c, 1.0), (self.ones_r, 1.0), (self.onesf_r, 1.0), (self.onesf_c, 1.0),
                  (self.epsc, EPS), (self.onefc, 1.0), (self.lnscc, LNSC),
                  (self.ut, 1.0), (self.trimask4, 1.0),
                  (self.idn, 1.0), (self.S, 0.0), (self.S16, 0.0),
                  (self.bagg, 0.0)]
        for t, val in consts:
            self.em(g, lambda e, t=t, val=val: e.memset(t[:], val))
        self.inc(g, "sG")
        self.wait(g, "sG")
        self.em(g, lambda e: e.affine_select(self.ut[:], self.ut[:], pattern=[[1, 128]],
                                             compare_op=ALU.is_ge, fill=0.0, base=0,
                                             channel_multiplier=-1))
        self.em(g, lambda e: e.affine_select(self.idn[:], self.idn[:], pattern=[[1, 128]],
                                             compare_op=ALU.is_equal, fill=0.0, base=0,
                                             channel_multiplier=-1))
        self.em(g, lambda e: e.affine_select(
            self.trimask4[:].rearrange("p (j t) -> p j t", t=128),
            self.trimask4[:].rearrange("p (j t) -> p j t", t=128),
            pattern=[[0, 4], [1, 128]], compare_op=ALU.is_ge, fill=0.0, base=0,
            channel_multiplier=-1))
        self.inc(g, "sG")
        self.dma(g, self.v3(self.wkg1, self.RK),
                 self.Wkg1T.rearrange("(i p) r -> p i r", p=128), "sMD")
        self.dma(g, self.wkg2[:], self.Wkg2T, "sMD")
        self.dma(g, self.b2_sb[:], self.bkg2, "sMD")
        self.dma(g, self.bg_sb[:], self.bgi, "sMD")
        self.dma(g, self.bf_sb[:], self.bfi, "sMD")
        self.dma("sync", self.selc[:], self.selv, "sFLG")
        self.wait("tensor", "sMD")
        w3 = self.v3(self.wkg1, self.RK)
        for i in range(self.NI):
            self.em("tensor", lambda e, i=i, w3=w3: e.matmul(
                self.pst[0:1, 0:self.RK], self.ones_c[:], w3[:, i, :],
                start=(i == 0), stop=(i == self.NI - 1)))
        self.inc("tensor", "sT")
        self.wait("vector", "sT")
        self.em("vector", lambda e: e.tensor_copy(self.wsum["kg1"][0:1, 0:self.RK],
                                                  self.pst[0:1, 0:self.RK]))
        self.inc("vector", "sV")

    def emit_ffn_pass1(self, og):
        NI, SL = self.NI, self.SL
        W = NI * SL
        self.wait("sync", "sV")
        tgt = self.dma("sync", self.xf[:, 0:W].rearrange("p (i o) -> p i o", o=SL),
                       self.WfT[:, og * SL:(og + 1) * SL]
                       .rearrange("(i p) o -> p i o", p=128), "sXF")
        self.wait("vector", "sXF", tgt)
        self.em("vector", lambda e: e.reduce_sum(self.bagg[:, 1:2], self.xf[:, 0:W],
                                                 axis=AX.X, apply_absolute_value=True))
        self.inc("vector", "sV")
        self.wait("vector", "sV")
        self.em("vector", lambda e: e.tensor_add(self.bagg[:, 0:1], self.bagg[:, 0:1],
                                                 self.bagg[:, 1:2]))
        self.inc("vector", "sV")

    def emit_ffn_scale(self):
        self.wait("sync", "sV")
        self.pgather(self.bagg[:, 0:1], self.fs_r[0:1, 0:128], 128)
        self.wait("vector", "sMP2")
        self.em("vector", lambda e: e.reduce_sum(self.fs_r[0:1, 128:129],
                                                 self.fs_r[0:1, 0:128], axis=AX.X))
        self.inc("vector", "sV")
        self.wait("vector", "sV")
        DD = float(self.D) * float(self.D)
        self.em("vector", lambda e: e.tensor_scalar(self.fs_r[0:1, 129:130],
                                                    self.fs_r[0:1, 128:129], 1.0 / DD,
                                                    None, op0=ALU.mult))
        self.em("vector", lambda e: e.tensor_scalar(self.fs_r[0:1, 130:131],
                                                    self.fs_r[0:1, 128:129], 0.5 / DD,
                                                    0.5 * FEPS, op0=ALU.mult,
                                                    op1=ALU.add))
        self.inc("vector", "sV")
        self.wait("scalar", "sV")
        self.em("scalar", lambda e: e.activation(self.lnS_p[:], self.fs_r[0:1, 129:130],
                                                 AF.Ln))
        self.em("scalar", lambda e: e.activation(self.lnS_n[:], self.lnS_p[:], AF.Copy,
                                                 scale=-1.0))
        self.inc("scalar", "sS")
        self.wait("tensor", "sV")
        self.em("tensor", lambda e: e.matmul(self.pA[:, 0:1], self.onesf_r[:],
                                             self.fs_r[0:1, 130:131], start=True,
                                             stop=True))
        self.inc("tensor", "sT")
        self.wait("vector", "sT")
        self.em("vector", lambda e: e.tensor_copy(self.shalf[:], self.pA[:, 0:1]))
        self.inc("vector", "sV")
        self.wait("vector", "sV")
        self.em("vector", lambda e: e.tensor_scalar(self.nshalf[:], self.shalf[:], -1.0,
                                                    None, op0=ALU.mult))
        self.shalf_v = self.inc("vector", "sV")

    def emit_ffn_pass2(self, og):
        NI, SL = self.NI, self.SL
        W = NI * SL
        self.wait("sync", "sFW")
        self.wait("sync", "sT")
        self.wait("sync", "sV")
        tgt = self.dma("sync", self.xf[:, 0:W].rearrange("p (i o) -> p i o", o=SL),
                       self.WfT[:, og * SL:(og + 1) * SL]
                       .rearrange("(i p) o -> p i o", p=128), "sXF")
        self.wait("vector", "sXF", tgt)
        self.wait("vector", "sV", self.shalf_v)
        self.em("vector", lambda e: e.tensor_scalar(self.sg[:, 0:W], self.xf[:, 0:W],
                                                    self.shalf[:, 0:1], None,
                                                    op0=ALU.is_ge))
        self.em("vector", lambda e: e.tensor_scalar(self.yfm[:, 0:W], self.xf[:, 0:W],
                                                    self.nshalf[:, 0:1], None,
                                                    op0=ALU.is_le))
        self.inc("vector", "sV")
        self.wait("vector", "sV")
        self.em("vector", lambda e: e.tensor_sub(self.sg[:, 0:W], self.sg[:, 0:W],
                                                 self.yfm[:, 0:W]))
        self.inc("vector", "sV")
        self.wait("tensor", "sV")
        a3 = self.sg[:, 0:W].rearrange("p (i t) -> p i t", t=SL)
        for i in range(NI):
            self.em("tensor", lambda e, i=i, a3=a3: e.matmul(
                self.pst[0:1, 0:SL], self.ones_c[:], a3[:, i, :], start=(i == 0),
                stop=(i == NI - 1)))
        self.inc("tensor", "sT")
        self.wait("vector", "sT")
        self.em("vector", lambda e, og=og: e.tensor_copy(
            self.wsum["t"][0:1, og * SL:(og + 1) * SL], self.pst[0:1, 0:SL]))
        self.inc("vector", "sV")
        self.wait("sync", "sV")
        self.dma("sync", self.wtsc[og], self.sg[:, 0:W], "sFW")

    def sq_acc(self, ap, idx, wv=None):
        TT = self.TT
        dst = self.sqacc if idx == 0 else (self.sqfA if idx % 2 else self.sqfB)
        if idx == 0:
            self.wait("scalar", "sV")
            self.wait("scalar", "sT")
        elif idx >= 2:
            self.wait("scalar", "sVo", self._sqv[idx % 2])
        if wv:
            self.wait("scalar", "sVo", wv)
        self.em("scalar", lambda e, ap=ap, dst=dst: e.activation(
            dst[:, 0:TT], ap, AF.Square))
        cs = self.cnt["sSo"]
        if idx > 0:
            self.wait("vector", "sSo", cs)
            self.em("vector", lambda e, dst=dst: e.tensor_add(
                self.sqacc[:, 0:TT], self.sqacc[:, 0:TT], dst[:, 0:TT]))
            self._sqv[idx % 2] = self.cnt["sVo"]

    def feat_stats(self, src3, inline_sq=True):
        NI, TT = self.NI, self.TT
        self.wait("tensor", "sV")
        self.wait("tensor", "sS")
        self.wait("tensor", "sSo", self.cnt["sSo"])
        for i in range(NI):
            self.em("tensor", lambda e, i=i, src3=src3: e.matmul(
                self.pst[0:1, 0:TT], self.ones_c[:], src3[:, i, 0:TT],
                start=(i == 0), stop=(i == NI - 1)))
        self.inc("tensor", "sT")
        if inline_sq:
            for i in range(NI):
                self.sq_acc(src3[:, i, 0:TT], i)
            self.inc("vector", "sV")
        self.wait("tensor", "sV")
        self.wait("tensor", "sSo", self.cnt["sSo"])
        self.em("tensor", lambda e: e.matmul(
            self.pst[64:65, 0:TT], self.onesf_c[:], self.sqacc[:, 0:TT],
            start=True, stop=True))
        self.inc("tensor", "sT")

    def stats_to_mu_var(self):
        D, TT = self.D, self.TT
        self.wait("vector", "sT")
        self.wait("vector", "sPR")
        self.wait("vector", "sMP1")
        self.em("vector", lambda e: e.tensor_scalar(self.mu_r, self.pst[0:1, 0:TT],
                                                    1.0 / D, None, op0=ALU.mult))
        self.em("vector", lambda e: e.tensor_mul(self.e2_r, self.mu_r, self.mu_r))
        self.em("vector", lambda e: e.scalar_tensor_tensor(
            self.e2_r, self.pst[64:65, 0:TT], 1.0 / D, self.e2_r,
            op0=ALU.mult, op1=ALU.subtract))
        self.inc("vector", "sV")

    def emit_ln1(self, m):
        D, TT, NCH = self.D, self.TT, self.NCH
        self.feat_stats(self.v3(self.xb))
        self.stats_to_mu_var()
        self.wait("vector", "sT")
        self.em("vector", lambda e: e.tensor_scalar(self.nmu1_r, self.pst[0:1, 0:TT],
                                                    -1.0 / D, None, op0=ALU.mult))
        self.inc("vector", "sV")
        self.wait("scalar", "sV")
        self.em("scalar", lambda e: e.activation(self.e2_r, self.e2_r, AF.Ln,
                                                 bias=self.epsc[0:1, 0:1]))
        self.inc("scalar", "sS")
        self.wait("scalar", "sS")
        self.wait("scalar", "sPR")
        self.wait("scalar", "sMP1")
        self.em("scalar", lambda e: e.activation(self.rA_r, self.e2_r, AF.Exp,
                                                 scale=-0.5))
        self.em("scalar", lambda e: e.activation(self.r1inv_r, self.e2_r, AF.Exp,
                                                 scale=0.5))
        self.inc("scalar", "sS")
        self.wait("sync", "sS")
        self.dma("sync", self.scrd[m, 0:TT], self.rA_r, "sPR")
        self.dma("sync", self.scrb[m, 0, 0:TT], self.nmu1_r, "sPR")
        self.dma("sync", self.scrb[m, 1, 0:TT], self.r1inv_r, "sPR")
        self.pscatter(self.rA_r, self.r1col[:, m * NCH:(m + 1) * NCH], TT)
        self.emit_r1b()

    def emit_r1b(self):
        self.wait("tensor", "sS")
        self.bcast_f32(self.rA_r, self.r1b[:, 0:self.TT])

    def emit_kg(self, m):
        D, DK, TT, NI, RK, NCH = self.D, self.DK, self.TT, self.NI, self.RK, self.NCH
        w3 = self.v3(self.wkg1, RK)
        xb3 = self.v3(self.xb)
        self.wait("tensor", "sV")
        for i in range(NI):
            self.em("tensor", lambda e, i=i, w3=w3, xb3=xb3: e.matmul(
                self.pb2[0:RK, 0:TT], w3[:, i, :], xb3[:, i, 0:TT],
                start=(i == 0), stop=False))
        self.em("tensor", lambda e: e.matmul(
            self.pb2[0:RK, 0:TT], self.wsum["kg1"][0:1, 0:RK], self.nmu1_r,
            start=False, stop=True))
        self.inc("tensor", "sT")
        self.wait("vector", "sT")
        self.em("vector", lambda e: e.tensor_mul(self.kg1s[0:RK, 0:TT],
                                                 self.pb2[0:RK, 0:TT],
                                                 self.r1b[0:RK, 0:TT]))
        kg1_v = self.inc("vector", "sV")
        ngrp = max(1, DK // 512)
        pps = [self.pA, self.pB]
        ups = [self.pb4, self.pb5]
        for c in range(NCH):
            self.wait("tensor", "sV", kg1_v)
            self.wait("tensor", "sS")
            for gg in range(ngrp):
                o0 = gg * 512
                oc = min(512, DK - o0)
                pp = pps[gg % 2]
                self.em("tensor", lambda e, c=c, o0=o0, oc=oc, pp=pp: e.matmul(
                    pp[:, 0:oc], self.kg1s[0:RK, c * 128:(c + 1) * 128],
                    self.wkg2[:, o0:o0 + oc], start=True, stop=False))
                self.em("tensor", lambda e, o0=o0, oc=oc, pp=pp: e.matmul(
                    pp[:, 0:oc], self.ones_r[0:1, 0:128], self.b2_sb[0:1, o0:o0 + oc],
                    start=False, stop=True))
            self.inc("tensor", "sT")
            self.wait("scalar", "sT")
            for gg in range(ngrp):
                o0 = gg * 512
                oc = min(512, DK - o0)
                self.em("scalar", lambda e, oc=oc, gg=gg: e.activation(
                    ups[gg % 2][:, 0:oc], pps[gg % 2][:, 0:oc], AF.Exp, scale=-1.0))
            self.inc("scalar", "sS")
            self.wait("scalar", "sS")
            for gg in range(ngrp):
                o0 = gg * 512
                oc = min(512, DK - o0)
                self.em("scalar", lambda e, c=c, o0=o0, oc=oc, gg=gg: e.activation(
                    self.gk[:, c * DK + o0:c * DK + o0 + oc], ups[gg % 2][:, 0:oc],
                    AF.Ln, bias=self.onefc[:, 0:1]))
            self.inc("scalar", "sS")
        self.wait("tensor", "sS")

    def emit_qk_proj(self, m, key, srcT, dest):
        DK, TT, NI, SL, HB, NQ = self.DK, self.TT, self.NI, self.SL, self.HB, self.NQ
        xb3 = self.v3(self.xb)
        ptc = [self.pA, self.pB, self.pb2, self.pb3]
        ocols = min(SL, DK)
        noc = ocols // 128
        for og in range(max(1, DK // SL)):
            bufs = []
            for h in range(NQ):
                bufs.append(self.slab_load(self.whalf_src(srcT, og, h, ocols),
                                           cast=True, ocols=ocols,
                                           cache_key=(key, og, h), m0=(m == 0)))
            self.wait("tensor", "sV")
            self.wait("tensor", "sS")
            if m == 0:
                self.emit_slab_wsum(bufs, key, og, ocols)
            for h in range(NQ):
                buf, tgt, slot = bufs[h]
                self.wait("tensor", tgt[0], tgt[1])
                b3 = buf[:, 0:HB * ocols].rearrange("p (i t) -> p i t", t=ocols)
                for i in range(HB):
                    for oc in range(noc):
                        self.em("tensor", lambda e, i=i, oc=oc, b3=b3, h=h, xb3=xb3:
                                e.matmul(ptc[oc][:, 0:TT],
                                         b3[:, i, oc * 128:(oc + 1) * 128],
                                         xb3[:, h * HB + i, 0:TT],
                                         start=(h == 0 and i == 0), stop=False))
            for oc in range(noc):
                self.em("tensor", lambda e, og=og, oc=oc, key=key, ocols=ocols: e.matmul(
                    ptc[oc][:, 0:TT],
                    self.wsum[key][0:1, og * ocols + oc * 128:og * ocols + (oc + 1) * 128],
                    self.nmu1_r, start=False, stop=True))
            self.inc("tensor", "sT")
            self.wait("vector", "sT")
            for oc in range(noc):
                hp = og * noc + oc
                self.em("vector", lambda e, hp=hp, oc=oc, dest=dest: e.tensor_mul(
                    dest[:, hp * TT:(hp + 1) * TT], ptc[oc][:, 0:TT], self.r1b[:, 0:TT]))
            self.inc("vector", "sV")
            for h in range(NQ):
                self.slab_done(bufs[h][2])
        self.wait("tensor", "sV")

    def emit_slab_wsum(self, bufs, key, og, ocols):
        HB, NQ = self.HB, self.NQ
        for h in range(NQ):
            buf, tgt, slot = bufs[h]
            self.wait("tensor", tgt[0], tgt[1])
            b3 = buf[:, 0:HB * ocols].rearrange("p (i t) -> p i t", t=ocols)
            for i in range(HB):
                self.em("tensor", lambda e, i=i, b3=b3, h=h, ocols=ocols: e.matmul(
                    self.pst[0:1, 0:ocols], self.ones_c[:], b3[:, i, :],
                    start=(h == 0 and i == 0), stop=(h == NQ - 1 and i == HB - 1)))
        self.inc("tensor", "sT")
        self.wait("vector", "sT")
        self.em("vector", lambda e, og=og, key=key, ocols=ocols: e.tensor_copy(
            self.wsum[key][0:1, og * ocols:(og + 1) * ocols], self.pst[0:1, 0:ocols]))
        self.inc("vector", "sV")
        self.wait("tensor", "sV")

    def emit_vg_proj(self, m, key, srcT, is_g):
        D, TT, NI, SL, NCH, HB, NQ = (self.D, self.TT, self.NI, self.SL, self.NCH,
                                      self.HB, self.NQ)
        ptc = [self.pb2, self.pb3, self.pb4, self.pb5][:NCH]
        self.wait("vector", "sMP2")
        self.wait("scalar", "sMP2")
        if is_g and m == 0:
            self.wait("scalar", "sV", self.phase2_sg_guard)
            self.wait("scalar", "sFW")
        xb3 = self.v3(self.xb)
        for og in range(D // SL):
            bufs = []
            for h in range(NQ):
                bufs.append(self.slab_load(self.whalf_src(srcT, og, h, SL), cast=True,
                                           cache_key=(key, og, h), m0=(m == 0)))
            if m == 0:
                self.wait("tensor", "sV")
                self.emit_slab_wsum(bufs, key, og, SL)
            self.wait("tensor", "sV")
            self.wait("tensor", "sS")
            for h in range(NQ):
                buf, tgt, slot = bufs[h]
                self.wait("tensor", tgt[0], tgt[1])
                b3 = buf[:].rearrange("p (i t) -> p i t", t=SL)
                for i in range(HB):
                    for tc in range(NCH):
                        self.em("tensor", lambda e, i=i, tc=tc, b3=b3, h=h, xb3=xb3:
                                e.matmul(ptc[tc][:, 0:SL],
                                         xb3[:, h * HB + i, tc * 128:(tc + 1) * 128],
                                         b3[:, i, :], start=(h == 0 and i == 0),
                                         stop=False))
            c0 = m * TT
            for tc in range(NCH):
                self.em("tensor", lambda e, tc=tc, og=og, key=key: e.matmul(
                    ptc[tc][:, 0:SL], self.nmu1_r[0:1, tc * 128:(tc + 1) * 128],
                    self.wsum[key][0:1, og * SL:(og + 1) * SL],
                    start=False, stop=not is_g))
                if is_g:
                    self.em("tensor", lambda e, tc=tc, og=og: e.matmul(
                        ptc[tc][:, 0:SL], self.r1inv_r[0:1, tc * 128:(tc + 1) * 128],
                        self.bg_sb[0:1, og * SL:(og + 1) * SL], start=False, stop=True))
            self.inc("tensor", "sT")
            if is_g:
                self.wait("scalar", "sT")
                for tc in range(NCH):
                    self.em("scalar", lambda e, tc=tc, og=og, m=m: e.activation(
                        self.sg[:, tc * D + og * SL:tc * D + (og + 1) * SL],
                        ptc[tc][:, 0:SL], AF.Silu,
                        scale=self.r1col[:, m * self.NCH + tc:m * self.NCH + tc + 1]))
                self.inc("scalar", "sS")
            else:
                self.wait("vector", "sT")
                for tc in range(NCH):
                    self.em("vector", lambda e, tc=tc, og=og, m=m: e.tensor_scalar(
                        self.vtm[:, tc * D + og * SL:tc * D + (og + 1) * SL],
                        ptc[tc][:, 0:SL],
                        self.r1col[:, m * self.NCH + tc:m * self.NCH + tc + 1], None,
                        op0=ALU.mult))
                self.inc("vector", "sV")
            for h in range(NQ):
                self.slab_done(bufs[h][2])
        if is_g:
            self.wait("tensor", "sS")
        else:
            self.wait("tensor", "sV")

    def emit_chunk_tables(self, m, c, need_eB):
        DK, HP = self.DK, self.HP
        self.wait("tensor", "sS")
        self.wait("tensor", "sV")
        nh = max(1, HP // 2)
        for hp in range(HP):
            pp = self.pA if hp < nh else self.pB
            off = (hp % nh) * 128
            self.em("tensor", lambda e, hp=hp, c=c, pp=pp, off=off: e.matmul(
                pp[:, off:off + 128],
                self.gk[:, c * DK + hp * 128:c * DK + (hp + 1) * 128],
                self.ut[:], start=True, stop=True))
        self.inc("tensor", "sT")
        self.wait("scalar", "sT")
        half = nh * 128
        self.em("scalar", lambda e, half=half: e.activation(
            self.emB[:, 0:half], self.pA[:, 0:half], AF.Exp, scale=NINV,
            bias=self.lnscc[:, 0:1]))
        if HP > nh:
            self.em("scalar", lambda e, half=half: e.activation(
                self.emB[:, half:2 * half], self.pB[:, 0:half], AF.Exp, scale=NINV,
                bias=self.lnscc[:, 0:1]))
        if need_eB:
            self.em("scalar", lambda e, half=half: e.activation(
                self.eB[:, 0:half], self.pA[:, 0:half], AF.Exp, scale=-NINV))
            if HP > nh:
                self.em("scalar", lambda e, half=half: e.activation(
                    self.eB[:, half:2 * half], self.pB[:, 0:half], AF.Exp, scale=-NINV))
        pa3 = self.pA[:, 0:half].rearrange("p (h t) -> p h t", t=128)
        self.em("scalar", lambda e, pa3=pa3, nh=nh: e.activation(
            self.eA[:, 0:nh].rearrange("p (h x) -> p h x", x=1),
            pa3[:, :, 127:128], AF.Exp, scale=-NINV))
        if HP > nh:
            pb3_ = self.pB[:, 0:half].rearrange("p (h t) -> p h t", t=128)
            self.em("scalar", lambda e, pb3_=pb3_, nh=nh: e.activation(
                self.eA[:, nh:2 * nh].rearrange("p (h x) -> p h x", x=1),
                pb3_[:, :, 127:128], AF.Exp, scale=-NINV))
        self.inc("scalar", "sS")

    def emit_ktilde(self, m, c):
        HP, TT = self.HP, self.TT
        self.wait("vector", "sS")
        for hp in range(HP):
            self.em("vector", lambda e, hp=hp, c=c: e.tensor_mul(
                self.k_sb[:, hp * TT + c * 128:hp * TT + (c + 1) * 128],
                self.k_sb[:, hp * TT + c * 128:hp * TT + (c + 1) * 128],
                self.emB[:, hp * 128:(hp + 1) * 128]))
        self.inc("vector", "sV")

    def emit_ktm(self, m, c, fence=True):
        HP, TT = self.HP, self.TT
        self.wait("tensor", "sV")
        for hp in range(HP):
            self.em("tensor", lambda e, hp=hp, c=c: e.transpose(
                self.ptr[:, (hp % 8) * 128:(hp % 8 + 1) * 128],
                self.k_sb[:, hp * TT + c * 128:hp * TT + (c + 1) * 128], self.idn[:]))
        self.inc("tensor", "sT")
        self.wait("scalar", "sT")
        for hp in range(HP):
            self.em("scalar", lambda e, hp=hp: e.activation(
                self.ktm[:, hp * 128:(hp + 1) * 128],
                self.ptr[:, (hp % 8) * 128:(hp % 8 + 1) * 128], AF.Copy))
        self.inc("scalar", "sS")
        self.ktm_s = self.cnt["sS"]
        if fence:
            self.wait("tensor", "sS")

    def emit_sup_wave(self, c, w):
        D, HP = self.D, self.HP
        hps = [hp for hp in range(4 * w, min(HP, 4 * w + 4))]
        pban = [self.pb2, self.pb3]
        for j, hp in enumerate(hps):
            pb = pban[j // 2]
            off = (j % 2) * 256
            self.em("tensor", lambda e, hp=hp, c=c, pb=pb, off=off: e.matmul(
                pb[:, off:off + 256], self.ktm[:, hp * 128:(hp + 1) * 128],
                self.vtm[:, c * D + hp * 256:c * D + (hp + 1) * 256],
                start=True, stop=True))
        self.inc("tensor", "sT")
        self.wait("vector", "sT")
        for j, hp in enumerate(hps):
            pb = pban[j // 2]
            off = (j % 2) * 256
            self.em("vector", lambda e, hp=hp, pb=pb, off=off: e.tensor_add(
                self.S[0:64, hp * 128:(hp + 1) * 128],
                self.S[0:64, hp * 128:(hp + 1) * 128], pb[0:64, off:off + 128]))
            self.em("vector", lambda e, hp=hp, pb=pb, off=off: e.tensor_add(
                self.S[64:128, hp * 128:(hp + 1) * 128],
                self.S[64:128, hp * 128:(hp + 1) * 128],
                pb[64:128, off + 128:off + 256]))
        for hp in hps:
            self.em("vector", lambda e, hp=hp: e.tensor_scalar(
                self.S[:, hp * 128:(hp + 1) * 128], self.S[:, hp * 128:(hp + 1) * 128],
                self.eA[:, hp:hp + 1], None, op0=ALU.mult))
        self.inc("vector", "sV")
        self.wait("tensor", "sV")

    def emit_attn_chunk(self, m, c):
        HP, TT, D, H = self.HP, self.TT, self.D, self.H
        self.wait("vector", "sCW")
        self.wait("vector", "sS")
        for hp in range(HP):
            self.em("vector", lambda e, hp=hp, c=c: e.tensor_mul(
                self.qt_sb[:, hp * 128:(hp + 1) * 128],
                self.q_sb[:, hp * TT + c * 128:hp * TT + (c + 1) * 128],
                self.eB[:, hp * 128:(hp + 1) * 128]))
        v_qt = self.inc("vector", "sV")
        self.wait("tensor", "sV", v_qt)
        nw = (HP + 3) // 4
        for w in range(nw):
            hps = [hp for hp in range(4 * w, min(HP, 4 * w + 4))]
            for j, hp in enumerate(hps):
                self.em("tensor", lambda e, hp=hp, c=c, j=j: e.matmul(
                    self.pb2[:, j * 128:(j + 1) * 128],
                    self.k_sb[0:64, hp * TT + c * 128:hp * TT + (c + 1) * 128],
                    self.qt_sb[0:64, hp * 128:(hp + 1) * 128], start=True, stop=True))
                self.em("tensor", lambda e, hp=hp, c=c, j=j: e.matmul(
                    self.pb3[:, j * 128:(j + 1) * 128],
                    self.k_sb[64:128, hp * TT + c * 128:hp * TT + (c + 1) * 128],
                    self.qt_sb[64:128, hp * 128:(hp + 1) * 128], start=True, stop=True))
            self.inc("tensor", "sT")
            self.wait("vector", "sT")
            wd = 128 * len(hps)
            self.em("vector", lambda e, wd=wd: e.tensor_mul(
                self.ptm[:, 0:wd], self.pb2[:, 0:wd], self.trimask4[:, 0:wd]))
            self.em("vector", lambda e, wd=wd: e.tensor_mul(
                self.ptm[:, 512:512 + wd], self.pb3[:, 0:wd], self.trimask4[:, 0:wd]))
            self.inc("vector", "sV")
            self.wait("tensor", "sV")
            pban = [self.pb4, self.pb5]
            for j, hp in enumerate(hps):
                pb = pban[j // 2]
                off = (j % 2) * 256
                self.em("tensor", lambda e, hp=hp, c=c, j=j, pb=pb, off=off: e.matmul(
                    pb[:, off:off + 128], self.ptm[:, j * 128:(j + 1) * 128],
                    self.vtm[:, c * D + hp * 256:c * D + hp * 256 + 128],
                    start=True, stop=False))
                self.em("tensor", lambda e, hp=hp, pb=pb, off=off: e.matmul(
                    pb[:, off:off + 128], self.qt_sb[0:64, hp * 128:(hp + 1) * 128],
                    self.S16[0:64, hp * 128:(hp + 1) * 128], start=False, stop=True))
                self.em("tensor", lambda e, hp=hp, c=c, j=j, pb=pb, off=off: e.matmul(
                    pb[:, off + 128:off + 256],
                    self.ptm[:, 512 + j * 128:512 + (j + 1) * 128],
                    self.vtm[:, c * D + hp * 256 + 128:c * D + (hp + 1) * 256],
                    start=True, stop=False))
                self.em("tensor", lambda e, hp=hp, pb=pb, off=off: e.matmul(
                    pb[:, off + 128:off + 256],
                    self.qt_sb[64:128, hp * 128:(hp + 1) * 128],
                    self.S16[64:128, hp * 128:(hp + 1) * 128], start=False, stop=True))
            self.inc("tensor", "sT")
            t_o = self.cnt["sT"]
            self.wait("tensor", "sS", self.ktm_s)
            pban2 = [self.pb2, self.pb3]
            for j, hp in enumerate(hps):
                pb = pban2[j // 2]
                off = (j % 2) * 256
                self.em("tensor", lambda e, hp=hp, c=c, pb=pb, off=off: e.matmul(
                    pb[:, off:off + 256], self.ktm[:, hp * 128:(hp + 1) * 128],
                    self.vtm[:, c * D + hp * 256:c * D + (hp + 1) * 256],
                    start=True, stop=True))
            self.inc("tensor", "sT")
            self.wait("scalar", "sT", t_o)
            h0 = hps[0]
            self.em("scalar", lambda e, h0=h0: e.activation(
                self.otm[:, h0 * 256:h0 * 256 + 512], self.pb4[:, 0:512], AF.Copy))
            if len(hps) > 2:
                self.em("scalar", lambda e, h0=h0: e.activation(
                    self.otm[:, h0 * 256 + 512:h0 * 256 + 1024], self.pb5[:, 0:512],
                    AF.Copy))
            self.inc("scalar", "sS")
            self.wait("vector", "sT")
            for j, hp in enumerate(hps):
                pb = pban2[j // 2]
                off = (j % 2) * 256
                self.em("vector", lambda e, hp=hp, pb=pb, off=off: e.tensor_add(
                    self.S[0:64, hp * 128:(hp + 1) * 128],
                    self.S[0:64, hp * 128:(hp + 1) * 128], pb[0:64, off:off + 128]))
                self.em("vector", lambda e, hp=hp, pb=pb, off=off: e.tensor_add(
                    self.S[64:128, hp * 128:(hp + 1) * 128],
                    self.S[64:128, hp * 128:(hp + 1) * 128],
                    pb[64:128, off + 128:off + 256]))
            for hp in hps:
                self.em("vector", lambda e, hp=hp: e.tensor_scalar(
                    self.S[:, hp * 128:(hp + 1) * 128],
                    self.S[:, hp * 128:(hp + 1) * 128],
                    self.eA[:, hp:hp + 1], None, op0=ALU.mult))
            self.inc("vector", "sV")
            wdd = 128 * len(hps)
            self.wait("scalar", "sV")
            self.em("scalar", lambda e, h0=h0, wdd=wdd: e.activation(
                self.S16[:, h0 * 128:h0 * 128 + wdd],
                self.S[:, h0 * 128:h0 * 128 + wdd], AF.Copy))
            self.inc("scalar", "sS")
            self.wait("vector", "sS")
            for hh in range(2 * h0, 2 * h0 + 2 * len(hps)):
                self.em("vector", lambda e, hh=hh: e.bn_stats(
                    self.bns[:, hh * 6:(hh + 1) * 6],
                    self.otm[:, hh * 128:(hh + 1) * 128]))
            self.inc("vector", "sV")
            h00 = 2 * h0
            nh = 2 * len(hps)
            m1 = self.bns[:, h00 * 6 + 1:(h00 + nh) * 6:6]
            m2 = self.bns[:, h00 * 6 + 4:(h00 + nh) * 6:6]
            cv1 = self.bns[:, h00 * 6 + 2:(h00 + nh) * 6:6]
            cv2 = self.bns[:, h00 * 6 + 5:(h00 + nh) * 6:6]
            hmu = self.bagg[:, h00:h00 + nh]
            hva = self.bagg[:, H + h00:H + h00 + nh]
            hrw = self.hr[:, h00:h00 + nh]
            self.em("vector", lambda e, hmu=hmu, m1=m1, m2=m2: e.tensor_add(hmu, m1,
                                                                            m2))
            self.em("vector", lambda e, hva=hva, cv1=cv1, cv2=cv2: e.tensor_add(
                hva, cv1, cv2))
            self.em("vector", lambda e, hrw=hrw, m1=m1, m2=m2: e.tensor_sub(hrw, m1,
                                                                            m2))
            self.em("vector", lambda e, hmu=hmu: e.tensor_scalar(hmu, hmu, 0.5, None,
                                                                 op0=ALU.mult))
            self.em("vector", lambda e, hva=hva: e.tensor_scalar(
                hva, hva, 1.0 / 128.0, None, op0=ALU.mult))
            self.em("vector", lambda e, hrw=hrw: e.tensor_mul(hrw, hrw, hrw))
            self.em("vector", lambda e, hva=hva, hrw=hrw: e.scalar_tensor_tensor(
                hva, hrw, 0.25, hva, op0=ALU.mult, op1=ALU.add))
            self.inc("vector", "sV")
            self.wait("scalar", "sV")
            self.em("scalar", lambda e, hva=hva, hrw=hrw: e.activation(
                hrw, hva, AF.Ln, bias=self.epsc[:, 0:1]))
            self.em("scalar", lambda e, hrw=hrw: e.activation(hrw, hrw, AF.Exp,
                                                              scale=-0.5))
            self.inc("scalar", "sS")
            self.wait("vector", "sS")
            for h in range(h00, h00 + nh):
                self.em("vector", lambda e, h=h: e.tensor_scalar(
                    self.otm[:, h * 128:(h + 1) * 128],
                    self.otm[:, h * 128:(h + 1) * 128],
                    self.bagg[:, h:h + 1], self.hr[:, h:h + 1],
                    op0=ALU.subtract, op1=ALU.mult))
            for h in range(h00, h00 + nh):
                self.em("vector", lambda e, h=h, c=c: e.tensor_mul(
                    self.otm[:, h * 128:(h + 1) * 128],
                    self.otm[:, h * 128:(h + 1) * 128],
                    self.sg[:, c * D + h * 128:c * D + (h + 1) * 128]))
            self.inc("vector", "sV")
            self.wait("tensor", "sV")
            for h in range(h00, h00 + nh):
                self.em("tensor", lambda e, h=h: e.transpose(
                    self.ptr[:, (h % 8) * 128:(h % 8 + 1) * 128],
                    self.otm[:, h * 128:(h + 1) * 128], self.idn[:]))
            self.inc("tensor", "sT")
            self.wait("scalar", "sT")
            for h in range(h00, h00 + nh):
                self.em("scalar", lambda e, h=h, c=c: e.activation(
                    self.yfm[:, h * TT + c * 128:h * TT + (c + 1) * 128],
                    self.ptr[:, (h % 8) * 128:(h % 8 + 1) * 128], AF.Copy))
            self.inc("scalar", "sS")
            self.wait("tensor", "sS")

    def emit_out_proj(self, m, is_ffn):
        D, TT, NI, SL, HB, NQ = self.D, self.TT, self.NI, self.SL, self.HB, self.NQ
        src3 = self.v3(self.yfm)
        ptc = [self.pA, self.pB, self.pb2, self.pb3]
        if is_ffn:
            self.wait("sync", "sFW", self.wtsc_done)
        elif m == 0:
            self.wait("vector", "sFW")
        for og in range(D // SL):
            bufs = []
            for h in range(NQ):
                if is_ffn:
                    bufs.append(self.slab_load(
                        self.wtsc[og].rearrange("p (i t) -> p i t", t=SL)
                        [:, h * HB:(h + 1) * HB, :], cast=False))
                else:
                    bufs.append(self.slab_load(self.whalf_src(self.WoT, og, h, SL),
                                               cast=True, cache_key=("o", og, h),
                                               m0=(m == 0)))
            self.wait("tensor", "sV")
            self.wait("tensor", "sS")
            for h in range(NQ):
                buf, tgt, slot = bufs[h]
                self.wait("tensor", tgt[0], tgt[1])
                b3 = buf[:].rearrange("p (i t) -> p i t", t=SL)
                for i in range(HB):
                    for oc in range(4):
                        last = (not is_ffn) and h == NQ - 1 and i == HB - 1
                        self.em("tensor", lambda e, i=i, oc=oc, b3=b3, h=h, src3=src3,
                                last=last: e.matmul(
                                    ptc[oc][:, 0:TT], b3[:, i, oc * 128:(oc + 1) * 128],
                                    src3[:, h * HB + i, 0:TT],
                                    start=(h == 0 and i == 0), stop=last))
            if is_ffn:
                for oc in range(4):
                    self.em("tensor", lambda e, og=og, oc=oc: e.matmul(
                        ptc[oc][:, 0:TT],
                        self.wsum["t"][0:1, og * SL + oc * 128:og * SL + (oc + 1) * 128],
                        self.nmu3_r, start=False, stop=False))
                    self.em("tensor", lambda e, og=og, oc=oc: e.matmul(
                        ptc[oc][:, 0:TT],
                        self.bf_sb[0:1, og * SL + oc * 128:og * SL + (oc + 1) * 128],
                        self.r3is_r, start=False, stop=True))
            self.inc("tensor", "sT")
            if is_ffn:
                self.wait("vector", "sT")
                for oc in range(4):
                    ic = og * 4 + oc
                    self.em("vector", lambda e, ic=ic, oc=oc: e.tensor_mul(
                        self.ao[:, ic * TT:(ic + 1) * TT], ptc[oc][:, 0:TT],
                        self.r3b[:, 0:TT]))
                    self.sq_acc(self.ao[:, ic * TT:(ic + 1) * TT], ic,
                                wv=self.cnt["sVo"])
                self.inc("vector", "sV")
            else:
                self.wait("scalar", "sT")
                for oc in range(4):
                    ic = og * 4 + oc
                    self.em("scalar", lambda e, ic=ic, oc=oc: e.activation(
                        self.ao[:, ic * TT:(ic + 1) * TT], ptc[oc][:, 0:TT],
                        AF.Copy))
                    self.sq_acc(ptc[oc][:, 0:TT], ic)
                self.inc("vector", "sV")
                self.wait("tensor", "sSo", self.cnt["sSo"])
            for h in range(NQ):
                self.slab_done(bufs[h][2])
        self.wait("tensor", "sV")
        self.wait("tensor", "sSo", self.cnt["sSo"])

    def emit_post_ln(self, m, cast_xb):
        self.emit_post_ln_head(m, cast_xb)
        self.emit_post_ln_tail(m, cast_xb)

    def emit_post_ln_head(self, m, cast_xb):
        D, TT, NI = self.D, self.TT, self.NI
        a3 = self.v3(self.ao)
        self.feat_stats(a3, inline_sq=False)
        self.stats_to_mu_var()
        self.wait("scalar", "sV")
        self.em("scalar", lambda e: e.activation(self.e2_r, self.e2_r, AF.Ln,
                                                 bias=self.epsc[0:1, 0:1]))
        self.em("scalar", lambda e: e.activation(self.rA_r, self.e2_r, AF.Exp,
                                                 scale=-0.5))
        self.inc("scalar", "sS")
        self.wait("vector", "sS")
        if cast_xb:
            self.em("vector", lambda e: e.tensor_mul(self.murA_r, self.mu_r,
                                                     self.rA_r))
        else:
            self.em("vector", lambda e: e.tensor_mul(self.mu_r, self.mu_r,
                                                     self.rA_r))
            self.em("vector", lambda e: e.tensor_add(self.mu_r, self.mu_r,
                                                     self.murA_r))
        self.inc("vector", "sV")
        self.wait("tensor", "sS")
        self.bcast_f32(self.rA_r, self.prb[:, 0:TT])
        if not cast_xb:
            self.wait("tensor", "sV")
            self.bcast_f32(self.mu_r, self.pmb[:, 0:TT])

    def emit_post_ln_tail(self, m, cast_xb):
        TT, NI = self.TT, self.NI
        a3 = self.v3(self.ao)
        xf3 = self.v3(self.xf)
        yf3 = self.v3(self.yfm)
        self.wait("scalar", "sT")
        for i in range(NI):
            self.em("vector", lambda e, i=i, a3=a3: e.tensor_mul(
                self.yrbb[:, 0:TT], a3[:, i, 0:TT], self.prb[:, 0:TT]))
            self.em("vector", lambda e, i=i, xf3=xf3: e.tensor_add(
                xf3[:, i, 0:TT], xf3[:, i, 0:TT], self.yrbb[:, 0:TT]))
            if cast_xb:
                cv = self.cnt["sVo"]
                self.wait("scalar", "sVo", cv)
                self.em("scalar", lambda e, i=i, xf3=xf3, yf3=yf3: e.activation(
                    yf3[:, i, 0:TT], xf3[:, i, 0:TT], AF.Copy))
                self.sq_acc(xf3[:, i, 0:TT], i, wv=cv)
            else:
                self.em("vector", lambda e, i=i, xf3=xf3: e.tensor_sub(
                    xf3[:, i, 0:TT], xf3[:, i, 0:TT], self.pmb[:, 0:TT]))
        self.inc("vector", "sV")

    def emit_ln3(self, m):
        D, TT = self.D, self.TT
        self.feat_stats(self.v3(self.yfm), inline_sq=False)
        self.stats_to_mu_var()
        self.wait("vector", "sT")
        self.em("vector", lambda e: e.tensor_scalar(self.nmu3_r, self.pst[0:1, 0:TT],
                                                    -1.0 / D, None, op0=ALU.mult))
        self.inc("vector", "sV")
        self.wait("scalar", "sV")
        self.em("scalar", lambda e: e.activation(self.e2_r, self.e2_r, AF.Ln,
                                                 bias=self.epsc[0:1, 0:1]))
        self.inc("scalar", "sS")
        self.wait("scalar", "sS")
        self.em("scalar", lambda e: e.activation(self.rA_r, self.e2_r, AF.Exp,
                                                 scale=-0.5, bias=self.lnS_p[0:1, 0:1]))
        self.em("scalar", lambda e: e.activation(self.r3is_r, self.e2_r, AF.Exp,
                                                 scale=0.5, bias=self.lnS_n[0:1, 0:1]))
        self.inc("scalar", "sS")
        self.wait("tensor", "sS")
        self.bcast_f32(self.rA_r, self.r3b[:, 0:TT])

    def emit_phase1_macro(self, m, guards):
        TT, NM = self.TT, self.NM
        if m == 0:
            xbt = self.dma("gpsimd", self.v3(self.xb),
                           self.xT[:, 0:TT].rearrange("(i p) t -> p i t", p=128),
                           "sXDg")
            self.wait("tensor", "sXDg", xbt)
            self.wait("vector", "sXDg", xbt)
            self.wait("scalar", "sXDg", xbt)
            self.emit_ln1(0)
        self.emit_kg(m)
        if guards.get("k_store"):
            self.wait("vector", "sKW", guards["k_store"])
        self.emit_qk_proj(m, "k", self.WkT, self.k_sb[:])
        if guards.get("v_store"):
            self.wait("vector", "sVW", guards["v_store"])
        self.emit_vg_proj(m, "v", self.WvT, is_g=False)
        guards["xb_free"] = (self.cnt["sT"], self.cnt["sV"])
        xbt = None
        if m + 1 < NM:
            self.wait("gpsimd", "sT", guards["xb_free"][0])
            self.wait("gpsimd", "sV", guards["xb_free"][1])
            xbt = self.dma("gpsimd", self.v3(self.xb),
                           self.xT[:, (m + 1) * TT:(m + 2) * TT]
                           .rearrange("(i p) t -> p i t", p=128), "sXDg")
        for c in range(self.NCH):
            self.emit_chunk_tables(m, c, need_eB=False)
            self.emit_ktilde(m, c)
            self.emit_ktm(m, c)
            self.wait("tensor", "sV")
            for w in range((self.HP + 3) // 4):
                self.emit_sup_wave(c, w)
        if m + 1 < NM:
            self.wait("tensor", "sXDg", xbt)
            self.wait("vector", "sXDg", xbt)
            self.wait("scalar", "sXDg", xbt)
            self.emit_ln1(m + 1)
        self.wait("sync", "sV")
        guards["k_store"] = self.dma("sync", self.ksc[m], self.k_sb[:], "sKW")
        guards["v_store"] = self.dma("sync", self.vsc[m], self.vtm[:], "sVW")

    def emit_state_handoff(self):
        HP = self.HP
        self.wait("sync", "sV")
        st = self.dma("sync", self.s_loc, self.S[:], "sMP1")
        self.wait("gpsimd", "sMP1", st)
        rg = [list(range(self.ncores))]
        self.em("gpsimd", lambda e, rg=rg: e.collective_compute(
            "AllGather", ALU.bypass, rg, [self.s_loc], [self.s_all]))
        self.inc("gpsimd", "sG")
        self.wait("sync", "sG")
        self.wait("vector", "sFLG")
        for r in range(self.ncores):
            tgt = self.dma("sync", self.Sbr_ap, self.s_all[r * 128:(r + 1) * 128, :],
                           "sPL")
            self.wait("vector", "sPL", tgt)
            if r == 0:
                self.em("vector", lambda e, r=r: e.tensor_scalar(
                    self.S[:], self.Sbr_ap, self.selc[:, r:r + 1], None, op0=ALU.mult))
            else:
                self.em("vector", lambda e, r=r: e.scalar_tensor_tensor(
                    self.S[:], self.Sbr_ap, self.selc[:, r:r + 1], self.S[:],
                    op0=ALU.mult, op1=ALU.add))
            self.inc("vector", "sV")
            if r < self.ncores - 1:
                self.wait("sync", "sV")
        self.wait("vector", "sV")
        self.em("vector", lambda e: e.tensor_copy(self.S16[:], self.S[:]))
        self.inc("vector", "sV")

    def emit_p2_proj(self, m, xbt):
        TT = self.TT
        self.wait("sync", "sT")
        self.wait("sync", "sPR")
        self.wait("sync", "sMP1")
        self.wait("sync", "sV")
        self.wait("sync", "sS")
        self.dma("sync", self.rA_r, self.scrd[m, 0:TT], "sPL")
        self.dma("sync", self.nmu1_r, self.scrb[m, 0, 0:TT], "sPL")
        pr3 = self.dma("sync", self.r1inv_r, self.scrb[m, 1, 0:TT], "sPL")
        self.wait("tensor", "sPL", pr3)
        self.wait("vector", "sPL", pr3)
        self.wait("scalar", "sPL", pr3)
        self.wait("tensor", "sXDg", xbt)
        self.wait("vector", "sXDg", xbt)
        self.emit_r1b()
        self.emit_qk_proj(m, "q", self.WqT, self.q_sb[:])
        self.emit_vg_proj(m, "g", self.WgT, is_g=True)
        self.emit_kg(m)
        self.p2_xb_rd = (self.cnt["sT"], self.cnt["sV"], self.cnt["sS"])

    def emit_phase2_macro(self, m, guards):
        TT, NM, NCH = self.TT, self.NM, self.NCH
        if guards.get("out_store"):
            self.wait("sync", "sOUT", guards["out_store"])
        if m == 0:
            self.wait("sync", "sS", self.phase2_xf_guard[0])
            self.wait("sync", "sV", self.phase2_xf_guard[1])
        xft = self.dma("sync", self.v3(self.xf),
                       self.xT[:, m * TT:(m + 1) * TT].rearrange("(i p) t -> p i t",
                                                                 p=128), "sXF")
        xbt = None
        if m + 1 < NM:
            self.wait("gpsimd", "sT", self.p2_xb_rd[0])
            self.wait("gpsimd", "sV", self.p2_xb_rd[1])
            self.wait("gpsimd", "sS", self.p2_xb_rd[2])
            xbt = self.dma("gpsimd", self.v3(self.xb),
                           self.xT[:, (m + 1) * TT:(m + 2) * TT]
                           .rearrange("(i p) t -> p i t", p=128), "sXDg")
        if guards.get("kv_free"):
            self.wait("sync", "sT", guards["kv_free"][0])
            self.wait("sync", "sV", guards["kv_free"][1])
        self.wait("sync", "sKW")
        self.wait("sync", "sVW")
        kst = self.dma("sync", self.k_sb[:], self.ksc[m], "sKS")
        vst = self.dma("sync", self.vtm[:], self.vsc[m], "sVS")
        self.wait("tensor", "sKS", kst)
        self.wait("vector", "sKS", kst)
        self.wait("tensor", "sVS", vst)
        self.wait("vector", "sVS", vst)
        for c in range(NCH):
            self.emit_chunk_tables(m, c, need_eB=True)
            self.emit_ktm(m, c, fence=False)
            self.emit_attn_chunk(m, c)
        self.slab_free[3] = dict(sT=self.cnt["sT"], sV=self.cnt["sV"], sS=0,
                                 sCW=self.slab_free[3]["sCW"])
        self.emit_out_proj(m, is_ffn=False)
        self.wait("vector", "sXF", xft)
        self.emit_post_ln_head(m, cast_xb=True)
        if m + 1 < NM:
            self.emit_p2_proj(m + 1, xbt)
        self.emit_post_ln_tail(m, cast_xb=True)
        self.emit_ln3(m)
        self.emit_out_proj(m, is_ffn=True)
        self.emit_post_ln(m, cast_xb=False)
        self.wait("sync", "sV")
        guards["out_store"] = self.dma(
            "sync",
            self.outT[:, m * TT:(m + 1) * TT].rearrange("(i p) t -> p i t", p=128),
            self.v3(self.xf), "sOUT")
        guards["kv_free"] = (self.cnt["sT"], self.cnt["sV"])

    def emit_all(self):
        self.emit_setup()
        guards = {}
        NM = self.NM
        nslab_f = self.D // self.SL
        for m in range(NM):
            if m < nslab_f:
                self.emit_ffn_pass1(m)
            if m == nslab_f:
                self.emit_ffn_scale()
            self.emit_phase1_macro(m, guards)
            og = m - (NM - nslab_f)
            if og >= 0:
                self.emit_ffn_pass2(og)
        self.wtsc_done = self.cnt["sFW"]
        self.phase2_sg_guard = self.cnt["sV"]
        self.wait("gpsimd", "sT", guards["xb_free"][0])
        self.wait("gpsimd", "sV", guards["xb_free"][1])
        xbt0 = self.dma("gpsimd", self.v3(self.xb),
                        self.xT[:, 0:self.TT].rearrange("(i p) t -> p i t", p=128),
                        "sXDg")
        self.emit_p2_proj(0, xbt0)
        self.emit_state_handoff()
        self.phase2_xf_guard = (self.cnt["sS"], self.cnt["sV"])
        guards = {"kv_free": (self.cnt["sT"], self.cnt["sV"]),
                  "out_store": 0}
        for m in range(NM):
            self.emit_phase2_macro(m, guards)
        self.wait("sync", "sOUT")
        self.wait("sync", "sKW")
        self.wait("sync", "sVW")
        self.wait("sync", "sL")

    def finalize(self):
        blk = self.ctx.enter_context(self.nc.Block())

        def run(e, eng):
            for item in self.prog[eng]:
                if item[0] == "w":
                    e.wait_ge(item[1], item[2])
                elif item[0] == "op":
                    ins = item[1](e)
                    if item[2] is not None:
                        ins.then_inc(self.sem[item[2]], item[3])
        for eng in ("sync", "gpsimd", "vector", "scalar", "tensor"):
            getattr(blk, eng)(lambda e, eng=eng: run(e, eng))
        self.ctx.close()
        return self.nc


def build(D=2048, T=4096, TT=512, RK=16, ncores=8):
    b = GlaBuilder(D=D, T=T, TT=TT, RK=RK, ncores=ncores)
    b.emit_all()
    return b.finalize()


def kernel(**inputs):
    x = np.asarray(inputs["L0_attn_x"], dtype=np.float32)
    Bn, L, D = x.shape
    T = L // 2
    n_cores = 8

    def tr(a):
        return np.ascontiguousarray(np.asarray(a, dtype=np.float32).T)

    wmap = dict(
        WqT=tr(inputs["L0_attn_Wq"]),
        WkT=tr(inputs["L0_attn_Wk"]),
        Wkg1T=tr(inputs["L0_attn_Wkg1"]),
        Wkg2T=tr(inputs["L0_attn_Wkg2"]),
        bkg2=np.asarray(inputs["L0_attn_bkg2"], dtype=np.float32).reshape(1, -1),
        WvT=tr(inputs["L0_attn_Wv"]),
        WgT=tr(inputs["L0_attn_Wg"]),
        bgi=np.asarray(inputs["L0_attn_bg"], dtype=np.float32).reshape(1, -1),
        WoT=tr(inputs["L0_attn_Wo"]),
        WfT=tr(inputs["L0_ffn_weight"]),
        bfi=np.asarray(inputs["L0_ffn_bias"], dtype=np.float32).reshape(1, -1),
    )
    nc = build(D=D, T=T)
    in_maps = []
    for c in range(n_cores):
        b_i, h_i = c // 2, c % 2
        mm = dict(wmap)
        mm["xT"] = np.ascontiguousarray(x[b_i, h_i * T:(h_i + 1) * T, :].T)
        mm["flag"] = np.full((128, 1), float(h_i), dtype=np.float32)
        sel = np.zeros((128, 8), dtype=np.float32)
        if h_i == 1:
            sel[:, c - 1] = 1.0
        mm["selv"] = sel
        in_maps.append(mm)
    res = run_bass_kernel_spmd(nc, in_maps, core_ids=list(range(n_cores)))
    out = np.empty_like(x)
    for c in range(n_cores):
        b_i, h_i = c // 2, c % 2
        out[b_i, h_i * T:(h_i + 1) * T, :] = res.results[c]["outT"].T
    return out


# version 17 (final)
# speedup vs baseline: 1.0166x; 1.0166x over previous
import math
import numpy as np
import concourse.bass as bass
import concourse.mybir as mybir
from concourse.bass_utils import run_bass_kernel_spmd
from contextlib import ExitStack

F32 = mybir.dt.float32
BF16 = mybir.dt.bfloat16
AF = mybir.ActivationFunctionType
ALU = mybir.AluOpType
AX = mybir.AxisListType

LNSC = -0.5 * math.log(128.0)
EPS = 1e-5
FEPS = 1e-10
NINV = 1.0 / 16.0


class GlaBuilder:
    def __init__(self, D=2048, T=4096, TT=512, RK=16, ncores=8):
        self.ncores = ncores
        DK = D // 2
        self.D, self.DK, self.T, self.TT, self.RK = D, DK, T, TT, RK
        self.NI = D // 128
        self.H = D // 128
        self.HP = self.H // 2
        self.NM = T // TT
        self.NCH = TT // 128
        self.SL = 512
        self.HB = max(1, self.NI // 4)
        self.NQ = self.NI // self.HB
        assert D % 512 == 0 and TT == 512 and T % TT == 0
        assert self.NM > self.D // self.SL
        self.nc = bass.Bass("TRN2", target_bir_lowering=False)
        self.ctx = ExitStack()
        self.prog = {k: [] for k in ("sync", "gpsimd", "vector", "scalar", "tensor")}
        self.sem = {}
        self.cnt = {}
        self._declare()

    def em(self, eng, fn):
        if eng in ("vector", "scalar"):
            so = "sVo" if eng == "vector" else "sSo"
            su = "sV" if eng == "vector" else "sS"
            self.wait(eng, so)
            self.wait(eng, su)
            entry = ["op", fn, so, 1]
            self.prog[eng].append(entry)
            self.cnt[so] += 1
        else:
            self.prog[eng].append(["op", fn, None, 0])

    def inc(self, eng, s, k=1):
        entry = self.prog[eng][-1]
        assert entry[0] == "op"
        if entry[2] is not None:
            self.cnt[entry[2]] -= entry[3]
        entry[2] = s
        entry[3] = k
        self.cnt[s] += k
        return self.cnt[s]

    def wait(self, eng, s, v=None):
        v = self.cnt[s] if v is None else v
        if v > 0 and self.lastw.setdefault(eng, {}).get(s, 0) < v:
            self.lastw[eng][s] = v
            h = self.sem[s]
            self.prog[eng].append(["w", h, v])

    def dma(self, q, out, in_, s, slow=False):
        h = self.sem[s]
        self.prog[q].append(["op", lambda e, out=out, in_=in_, slow=slow:
                             e.dma_start(out=out, in_=in_,
                                         allow_slow_non_contiguous=slow), s, 16])
        self.cnt[s] += 16
        return self.cnt[s]

    def _declare(self):
        nc, ctx = self.nc, self.ctx
        D, DK, T, TT, RK = self.D, self.DK, self.T, self.TT, self.RK
        NI, H, HP, NM, NCH, SL, HB = (self.NI, self.H, self.HP, self.NM, self.NCH,
                                      self.SL, self.HB)

        def din(name, shape):
            return nc.dram_tensor(name, shape, F32, kind="ExternalInput").ap()
        self.xT = din("xT", [D, T])
        self.WqT = din("WqT", [D, DK])
        self.WkT = din("WkT", [D, DK])
        self.Wkg1T = din("Wkg1T", [D, RK])
        self.Wkg2T = din("Wkg2T", [RK, DK])
        self.bkg2 = din("bkg2", [1, DK])
        self.WvT = din("WvT", [D, D])
        self.WgT = din("WgT", [D, D])
        self.bgi = din("bgi", [1, D])
        self.WoT = din("WoT", [D, D])
        self.WfT = din("WfT", [D, D])
        self.bfi = din("bfi", [1, D])
        self.flag = din("flag", [128, 1])
        self.selv = din("selv", [128, 8])
        self.outT = nc.dram_tensor("outT", [D, T], F32, kind="ExternalOutput").ap()
        self.ksc = nc.dram_tensor("ksc", [NM, 128, HP * TT], BF16, kind="Internal").ap()
        self.vsc = nc.dram_tensor("vsc", [NM, 128, NCH * D], BF16, kind="Internal").ap()
        self.wtsc = nc.dram_tensor("wtsc", [D // SL, 128, NI * SL], BF16,
                                   kind="Internal").ap()
        self.scrd = nc.dram_tensor("scrd", [NM + 1, TT], F32, kind="Internal").ap()
        nqt = 4 * (DK // SL if DK >= SL else 1) * 2 + 4 * (D // SL) * 3
        self.wsc = nc.dram_tensor("wsc", [nqt, 128, HB * SL], BF16, kind="Internal").ap()
        self.s_loc = nc.dram_tensor("s_loc", [128, HP * 128], F32, kind="Internal").ap()
        self.s_all = nc.dram_tensor("s_all", [self.ncores * 128, HP * 128], F32, kind="Internal").ap()
        self.scrb = nc.dram_tensor("scrb", [NM, 2, TT], BF16, kind="Internal").ap()

        def sb(name, shape, dt=F32):
            t = ctx.enter_context(nc.sbuf_tensor(name, shape, dt))
            setattr(self, name, t)
            return t
        sb("otm", [128, H * 128], BF16)
        self.slabs = [sb(f"slab{i}", [128, HB * SL], BF16) for i in range(3)] + [self.otm]
        sb("xb", [128, NI * TT], BF16)
        sb("xf", [128, NI * TT], F32)
        sb("k_sb", [128, HP * TT], BF16)
        sb("q_sb", [128, HP * TT], BF16)
        sb("vtm", [128, NCH * D], BF16)
        self.ao = self.vtm
        sb("gk", [128, NCH * DK], BF16)
        sb("sg", [128, NCH * D], BF16)
        sb("yfm", [128, NI * TT], BF16)
        sb("kg1s", [RK, TT], BF16)
        sb("ktm", [128, HP * 128], BF16)
        sb("qt_sb", [128, HP * 128], BF16)
        sb("ptm", [128, 1024], BF16)
        sb("sqacc", [128, TT], F32)
        sb("eB", [128, HP * 128], BF16)
        sb("emB", [128, HP * 128], BF16)
        sb("eA", [128, HP], F32)
        sb("S", [128, HP * 128], F32)
        sb("S16", [128, HP * 128], BF16)
        sb("bns", [128, H * 6], F32)
        sb("bagg", [128, 2 * H], F32)
        sb("hr", [128, H], F32)
        sb("sqfA", [128, TT], F32)
        sb("sqfB", [128, TT], F32)
        sb("yrbb", [128, TT], F32)
        sb("r1col", [128, NM * NCH], F32)
        sb("svs", [128, 4 * TT], F32)
        sb("svb", [128, 2 * TT], BF16)
        sb("r1b", [128, TT], F32)
        sb("r3b", [128, TT], F32)
        sb("ut", [128, 128], BF16)
        sb("trimask4", [128, 512], F32)
        sb("idn", [128, 128], BF16)
        sb("ones_c", [128, 1], BF16)
        sb("ones_r", [1, 128], BF16)
        sb("onesf_r", [1, 128], F32)
        sb("onesf_c", [128, 1], F32)
        sb("selc", [128, 8], F32)
        sb("wkg1", [128, NI * RK], BF16)
        sb("wkg2", [RK, DK], BF16)
        sb("b2_sb", [1, DK], BF16)
        sb("bg_sb", [1, D], BF16)
        sb("bf_sb", [1, D], BF16)
        self.wsum = {}
        for m, w in (("q", DK), ("k", DK), ("kg1", RK), ("v", D), ("g", D), ("t", D)):
            self.wsum[m] = ctx.enter_context(nc.sbuf_tensor(f"wsum_{m}", [1, w], BF16))
        sb("epsc", [128, 1], F32)
        sb("onefc", [128, 1], F32)
        sb("lnscc", [128, 1], F32)
        sb("lnS_p", [1, 1], F32)
        sb("lnS_n", [1, 1], F32)
        sb("shalf", [128, 1], F32)
        sb("nshalf", [128, 1], F32)
        self.prb = self.r3b
        self.pmb = self.r1b
        self.Sbr_ap = self.xf[:, 0:HP * 128]
        self.mu_r = self.svs[0:1, 0:TT]
        self.e2_r = self.svs[0:1, TT:2 * TT]
        self.rA_r = self.svs[0:1, 2 * TT:3 * TT]
        self.murA_r = self.svs[0:1, 3 * TT:4 * TT]
        self.fs_r = self.yrbb[0:1, 0:192]
        self.nmu1_r = self.svb[0:1, 0:TT]
        self.r1inv_r = self.svb[0:1, TT:2 * TT]
        self.nmu3_r = self.svb[0:1, 0:TT]
        self.r3is_r = self.svb[0:1, TT:2 * TT]

        def psd(name, shape, dt=F32):
            return ctx.enter_context(nc.psum_tensor(name, shape, dt))
        self.pA = psd("pA", [128, 512])
        self.pB = psd("pB", [128, 512])
        self.pb2 = psd("pb2", [128, 512])
        self.pb3 = psd("pb3", [128, 512])
        self.pb4 = psd("pb4", [128, 512])
        self.pb5 = psd("pb5", [128, 512])
        self.ptr = psd("ptr", [128, 1024], BF16)
        self.pst = psd("pst", [128, 512])

        for s in ("sWDg0", "sWDg1", "sWDg2", "sWDg3", "sWDs0", "sWDs1", "sWDs2", "sWDs3", "sCW", "sXF", "sKS", "sVS", "sXDg",
                  "sOUT", "sKW", "sVW", "sFW", "sMD", "sMP1", "sMP2", "sFLG",
                  "sPR", "sPL", "sT", "sV", "sS", "sG", "sR", "sL", "sVo", "sSo"):
            self.sem[s] = ctx.enter_context(nc.semaphore(s))
            self.cnt[s] = 0
        self.lastw = {}
        self._sqv = [0, 0]
        self.pAB_s = 0
        self.pAB_v = 0
        self.slab_free = [dict(sT=0, sV=0, sS=0, sCW=0) for _ in range(4)]
        self.slab_i = 0
        self.wq_idx = {}
        self.wq_n = 0

    def v3(self, t, w=None):
        w = self.TT if w is None else w
        return t[:].rearrange("p (i t) -> p i t", t=w)

    def slab_load(self, src_ap, cast, ocols=None, cache_key=None, m0=False):
        ocols = self.SL if ocols is None else ocols
        if cache_key is not None:
            qi = self.wq_idx.setdefault(cache_key, len(self.wq_idx))
            if not m0:
                src_ap = self.wsc[qi][:, 0:self.HB * ocols].rearrange(
                    "p (i t) -> p i t", t=ocols)
                cast = False
        slot = self.slab_i % 4
        fr = self.slab_free[slot]
        q = "gpsimd" if cast else "sync"
        sem = ("sWDg" if cast else "sWDs") + str(slot)
        for s, v in fr.items():
            if v:
                self.wait(q, s, v)
        dst = self.slabs[slot][:, 0:self.HB * ocols].rearrange("p (i t) -> p i t",
                                                               t=ocols)
        tgt = (sem, self.dma(q, dst, src_ap, sem))
        if cache_key is not None and m0:
            self.wait("sync", sem, tgt[1])
            self.dma("sync", self.wsc[qi][:, 0:self.HB * ocols],
                     self.slabs[slot][:, 0:self.HB * ocols], "sCW")
        self.slab_i += 1
        return self.slabs[slot], tgt, slot

    def slab_done(self, slot):
        self.slab_free[slot] = dict(sT=self.cnt["sT"], sV=0, sS=0, sCW=self.cnt["sCW"])

    def whalf_src(self, srcT, og, h, ocols):
        HB = self.HB
        return (srcT[HB * 128 * h:HB * 128 * (h + 1), og * ocols:(og + 1) * ocols]
                .rearrange("(i p) o -> p i o", p=128))

    def bcast_f32(self, src_row, dst_tile):
        TT = self.TT
        self.wait("tensor", "sV")
        self.em("tensor", lambda e, src_row=src_row: e.matmul(
            self.pA[:, 0:TT], self.onesf_r[:], src_row, start=True, stop=True))
        self.inc("tensor", "sT")
        self.wait("vector", "sT")
        self.em("vector", lambda e, dst_tile=dst_tile: e.tensor_copy(dst_tile,
                                                                     self.pA[:, 0:TT]))
        self.inc("vector", "sV")

    def pscatter(self, src_row, dst_col, W):
        self.wait("sync", "sMP2")
        self.dma("sync", self.scrd[self.NM, 0:W], src_row, "sMP1")
        self.wait("sync", "sMP1")
        self.dma("sync", dst_col,
                 self.scrd[self.NM, 0:W].rearrange("(a p) -> p a", p=128), "sMP2",
                 slow=True)

    def pgather(self, src_col, dst_row, P):
        self.wait("sync", "sMP2")
        self.dma("sync", self.scrd[self.NM, 0:P].rearrange("(p a) -> p a", a=1),
                 src_col, "sMP1", slow=True)
        self.wait("sync", "sMP1")
        self.dma("sync", dst_row,
                 self.scrd[self.NM, 0:P].rearrange("(a p) -> a p", a=1), "sMP2")

    def emit_setup(self):
        g = "gpsimd"
        consts = [(self.ones_c, 1.0), (self.ones_r, 1.0), (self.onesf_r, 1.0), (self.onesf_c, 1.0),
                  (self.epsc, EPS), (self.onefc, 1.0), (self.lnscc, LNSC),
                  (self.ut, 1.0), (self.trimask4, 1.0),
                  (self.idn, 1.0), (self.S, 0.0), (self.S16, 0.0),
                  (self.bagg, 0.0)]
        for t, val in consts:
            self.em(g, lambda e, t=t, val=val: e.memset(t[:], val))
        self.inc(g, "sG")
        self.wait(g, "sG")
        self.em(g, lambda e: e.affine_select(self.ut[:], self.ut[:], pattern=[[1, 128]],
                                             compare_op=ALU.is_ge, fill=0.0, base=0,
                                             channel_multiplier=-1))
        self.em(g, lambda e: e.affine_select(self.idn[:], self.idn[:], pattern=[[1, 128]],
                                             compare_op=ALU.is_equal, fill=0.0, base=0,
                                             channel_multiplier=-1))
        self.em(g, lambda e: e.affine_select(
            self.trimask4[:].rearrange("p (j t) -> p j t", t=128),
            self.trimask4[:].rearrange("p (j t) -> p j t", t=128),
            pattern=[[0, 4], [1, 128]], compare_op=ALU.is_ge, fill=0.0, base=0,
            channel_multiplier=-1))
        self.inc(g, "sG")
        self.dma(g, self.v3(self.wkg1, self.RK),
                 self.Wkg1T.rearrange("(i p) r -> p i r", p=128), "sMD")
        self.dma(g, self.wkg2[:], self.Wkg2T, "sMD")
        self.dma(g, self.b2_sb[:], self.bkg2, "sMD")
        self.dma(g, self.bg_sb[:], self.bgi, "sMD")
        self.dma(g, self.bf_sb[:], self.bfi, "sMD")
        self.dma("sync", self.selc[:], self.selv, "sFLG")
        self.wait("tensor", "sMD")
        w3 = self.v3(self.wkg1, self.RK)
        for i in range(self.NI):
            self.em("tensor", lambda e, i=i, w3=w3: e.matmul(
                self.pst[0:1, 0:self.RK], self.ones_c[:], w3[:, i, :],
                start=(i == 0), stop=(i == self.NI - 1)))
        self.inc("tensor", "sT")
        self.wait("vector", "sT")
        self.em("vector", lambda e: e.tensor_copy(self.wsum["kg1"][0:1, 0:self.RK],
                                                  self.pst[0:1, 0:self.RK]))
        self.inc("vector", "sV")

    def emit_ffn_pass1(self, og):
        NI, SL = self.NI, self.SL
        W = NI * SL
        self.wait("sync", "sV")
        tgt = self.dma("sync", self.xf[:, 0:W].rearrange("p (i o) -> p i o", o=SL),
                       self.WfT[:, og * SL:(og + 1) * SL]
                       .rearrange("(i p) o -> p i o", p=128), "sXF")
        self.wait("vector", "sXF", tgt)
        self.em("vector", lambda e: e.reduce_sum(self.bagg[:, 1:2], self.xf[:, 0:W],
                                                 axis=AX.X, apply_absolute_value=True))
        self.inc("vector", "sV")
        self.wait("vector", "sV")
        self.em("vector", lambda e: e.tensor_add(self.bagg[:, 0:1], self.bagg[:, 0:1],
                                                 self.bagg[:, 1:2]))
        self.inc("vector", "sV")

    def emit_ffn_scale(self):
        self.wait("sync", "sV")
        self.pgather(self.bagg[:, 0:1], self.fs_r[0:1, 0:128], 128)
        self.wait("vector", "sMP2")
        self.em("vector", lambda e: e.reduce_sum(self.fs_r[0:1, 128:129],
                                                 self.fs_r[0:1, 0:128], axis=AX.X))
        self.inc("vector", "sV")
        self.wait("vector", "sV")
        DD = float(self.D) * float(self.D)
        self.em("vector", lambda e: e.tensor_scalar(self.fs_r[0:1, 129:130],
                                                    self.fs_r[0:1, 128:129], 1.0 / DD,
                                                    None, op0=ALU.mult))
        self.em("vector", lambda e: e.tensor_scalar(self.fs_r[0:1, 130:131],
                                                    self.fs_r[0:1, 128:129], 0.5 / DD,
                                                    0.5 * FEPS, op0=ALU.mult,
                                                    op1=ALU.add))
        self.inc("vector", "sV")
        self.wait("scalar", "sV")
        self.em("scalar", lambda e: e.activation(self.lnS_p[:], self.fs_r[0:1, 129:130],
                                                 AF.Ln))
        self.em("scalar", lambda e: e.activation(self.lnS_n[:], self.lnS_p[:], AF.Copy,
                                                 scale=-1.0))
        self.inc("scalar", "sS")
        self.wait("tensor", "sV")
        self.em("tensor", lambda e: e.matmul(self.pA[:, 0:1], self.onesf_r[:],
                                             self.fs_r[0:1, 130:131], start=True,
                                             stop=True))
        self.inc("tensor", "sT")
        self.wait("vector", "sT")
        self.em("vector", lambda e: e.tensor_copy(self.shalf[:], self.pA[:, 0:1]))
        self.inc("vector", "sV")
        self.wait("vector", "sV")
        self.em("vector", lambda e: e.tensor_scalar(self.nshalf[:], self.shalf[:], -1.0,
                                                    None, op0=ALU.mult))
        self.shalf_v = self.inc("vector", "sV")

    def emit_ffn_pass2(self, og):
        NI, SL = self.NI, self.SL
        W = NI * SL
        self.wait("sync", "sFW")
        self.wait("sync", "sT")
        self.wait("sync", "sV")
        tgt = self.dma("sync", self.xf[:, 0:W].rearrange("p (i o) -> p i o", o=SL),
                       self.WfT[:, og * SL:(og + 1) * SL]
                       .rearrange("(i p) o -> p i o", p=128), "sXF")
        self.wait("vector", "sXF", tgt)
        self.wait("vector", "sV", self.shalf_v)
        self.em("vector", lambda e: e.tensor_scalar(self.sg[:, 0:W], self.xf[:, 0:W],
                                                    self.shalf[:, 0:1], None,
                                                    op0=ALU.is_ge))
        self.em("vector", lambda e: e.tensor_scalar(self.yfm[:, 0:W], self.xf[:, 0:W],
                                                    self.nshalf[:, 0:1], None,
                                                    op0=ALU.is_le))
        self.inc("vector", "sV")
        self.wait("vector", "sV")
        self.em("vector", lambda e: e.tensor_sub(self.sg[:, 0:W], self.sg[:, 0:W],
                                                 self.yfm[:, 0:W]))
        self.inc("vector", "sV")
        self.wait("tensor", "sV")
        a3 = self.sg[:, 0:W].rearrange("p (i t) -> p i t", t=SL)
        for i in range(NI):
            self.em("tensor", lambda e, i=i, a3=a3: e.matmul(
                self.pst[0:1, 0:SL], self.ones_c[:], a3[:, i, :], start=(i == 0),
                stop=(i == NI - 1)))
        self.inc("tensor", "sT")
        self.wait("vector", "sT")
        self.em("vector", lambda e, og=og: e.tensor_copy(
            self.wsum["t"][0:1, og * SL:(og + 1) * SL], self.pst[0:1, 0:SL]))
        self.inc("vector", "sV")
        self.wait("sync", "sV")
        self.dma("sync", self.wtsc[og], self.sg[:, 0:W], "sFW")

    def sq_acc(self, ap, idx, wv=None):
        TT = self.TT
        dst = self.sqacc if idx == 0 else (self.sqfA if idx % 2 else self.sqfB)
        if idx == 0:
            self.wait("scalar", "sV")
            self.wait("scalar", "sT")
        elif idx >= 2:
            self.wait("scalar", "sVo", self._sqv[idx % 2])
        if wv:
            self.wait("scalar", "sVo", wv)
        self.em("scalar", lambda e, ap=ap, dst=dst: e.activation(
            dst[:, 0:TT], ap, AF.Square))
        cs = self.cnt["sSo"]
        if idx > 0:
            self.wait("vector", "sSo", cs)
            self.em("vector", lambda e, dst=dst: e.tensor_add(
                self.sqacc[:, 0:TT], self.sqacc[:, 0:TT], dst[:, 0:TT]))
            self._sqv[idx % 2] = self.cnt["sVo"]

    def feat_stats(self, src3, inline_sq=True):
        NI, TT = self.NI, self.TT
        self.wait("tensor", "sV")
        self.wait("tensor", "sS")
        self.wait("tensor", "sSo", self.cnt["sSo"])
        for i in range(NI):
            self.em("tensor", lambda e, i=i, src3=src3: e.matmul(
                self.pst[0:1, 0:TT], self.ones_c[:], src3[:, i, 0:TT],
                start=(i == 0), stop=(i == NI - 1)))
        self.inc("tensor", "sT")
        if inline_sq:
            for i in range(NI):
                self.sq_acc(src3[:, i, 0:TT], i)
            self.inc("vector", "sV")
        self.wait("tensor", "sV")
        self.wait("tensor", "sSo", self.cnt["sSo"])
        self.em("tensor", lambda e: e.matmul(
            self.pst[64:65, 0:TT], self.onesf_c[:], self.sqacc[:, 0:TT],
            start=True, stop=True))
        self.inc("tensor", "sT")

    def stats_to_mu_var(self):
        D, TT = self.D, self.TT
        self.wait("vector", "sT")
        self.wait("vector", "sPR")
        self.wait("vector", "sMP1")
        self.em("vector", lambda e: e.tensor_scalar(self.mu_r, self.pst[0:1, 0:TT],
                                                    1.0 / D, None, op0=ALU.mult))
        self.em("vector", lambda e: e.tensor_mul(self.e2_r, self.mu_r, self.mu_r))
        self.em("vector", lambda e: e.scalar_tensor_tensor(
            self.e2_r, self.pst[64:65, 0:TT], 1.0 / D, self.e2_r,
            op0=ALU.mult, op1=ALU.subtract))
        self.inc("vector", "sV")

    def emit_ln1(self, m):
        D, TT, NCH = self.D, self.TT, self.NCH
        self.feat_stats(self.v3(self.xb))
        self.stats_to_mu_var()
        self.wait("vector", "sT")
        self.em("vector", lambda e: e.tensor_scalar(self.nmu1_r, self.pst[0:1, 0:TT],
                                                    -1.0 / D, None, op0=ALU.mult))
        self.inc("vector", "sV")
        self.wait("scalar", "sV")
        self.em("scalar", lambda e: e.activation(self.e2_r, self.e2_r, AF.Ln,
                                                 bias=self.epsc[0:1, 0:1]))
        self.inc("scalar", "sS")
        self.wait("scalar", "sS")
        self.wait("scalar", "sPR")
        self.wait("scalar", "sMP1")
        self.em("scalar", lambda e: e.activation(self.rA_r, self.e2_r, AF.Exp,
                                                 scale=-0.5))
        self.em("scalar", lambda e: e.activation(self.r1inv_r, self.e2_r, AF.Exp,
                                                 scale=0.5))
        self.inc("scalar", "sS")
        self.wait("sync", "sS")
        self.dma("sync", self.scrd[m, 0:TT], self.rA_r, "sPR")
        self.dma("sync", self.scrb[m, 0, 0:TT], self.nmu1_r, "sPR")
        self.dma("sync", self.scrb[m, 1, 0:TT], self.r1inv_r, "sPR")
        self.pscatter(self.rA_r, self.r1col[:, m * NCH:(m + 1) * NCH], TT)
        self.emit_r1b()

    def emit_r1b(self):
        self.wait("tensor", "sS")
        self.bcast_f32(self.rA_r, self.r1b[:, 0:self.TT])

    def emit_kg(self, m):
        D, DK, TT, NI, RK, NCH = self.D, self.DK, self.TT, self.NI, self.RK, self.NCH
        w3 = self.v3(self.wkg1, RK)
        xb3 = self.v3(self.xb)
        self.wait("tensor", "sV")
        for i in range(NI):
            self.em("tensor", lambda e, i=i, w3=w3, xb3=xb3: e.matmul(
                self.pb2[0:RK, 0:TT], w3[:, i, :], xb3[:, i, 0:TT],
                start=(i == 0), stop=False))
        self.em("tensor", lambda e: e.matmul(
            self.pb2[0:RK, 0:TT], self.wsum["kg1"][0:1, 0:RK], self.nmu1_r,
            start=False, stop=True))
        self.inc("tensor", "sT")
        self.wait("vector", "sT")
        self.em("vector", lambda e: e.tensor_mul(self.kg1s[0:RK, 0:TT],
                                                 self.pb2[0:RK, 0:TT],
                                                 self.r1b[0:RK, 0:TT]))
        kg1_v = self.inc("vector", "sV")
        ngrp = max(1, DK // 512)
        pps = [self.pA, self.pB]
        ups = [self.pb4, self.pb5]
        for c in range(NCH):
            self.wait("tensor", "sV", kg1_v)
            self.wait("tensor", "sS")
            for gg in range(ngrp):
                o0 = gg * 512
                oc = min(512, DK - o0)
                pp = pps[gg % 2]
                self.em("tensor", lambda e, c=c, o0=o0, oc=oc, pp=pp: e.matmul(
                    pp[:, 0:oc], self.kg1s[0:RK, c * 128:(c + 1) * 128],
                    self.wkg2[:, o0:o0 + oc], start=True, stop=False))
                self.em("tensor", lambda e, o0=o0, oc=oc, pp=pp: e.matmul(
                    pp[:, 0:oc], self.ones_r[0:1, 0:128], self.b2_sb[0:1, o0:o0 + oc],
                    start=False, stop=True))
            self.inc("tensor", "sT")
            self.wait("scalar", "sT")
            for gg in range(ngrp):
                o0 = gg * 512
                oc = min(512, DK - o0)
                self.em("scalar", lambda e, oc=oc, gg=gg: e.activation(
                    ups[gg % 2][:, 0:oc], pps[gg % 2][:, 0:oc], AF.Exp, scale=-1.0))
            self.inc("scalar", "sS")
            self.wait("scalar", "sS")
            for gg in range(ngrp):
                o0 = gg * 512
                oc = min(512, DK - o0)
                self.em("scalar", lambda e, c=c, o0=o0, oc=oc, gg=gg: e.activation(
                    self.gk[:, c * DK + o0:c * DK + o0 + oc], ups[gg % 2][:, 0:oc],
                    AF.Ln, bias=self.onefc[:, 0:1]))
            self.inc("scalar", "sS")
        self.wait("tensor", "sS")

    def emit_qk_proj(self, m, key, srcT, dest):
        DK, TT, NI, SL, HB, NQ = self.DK, self.TT, self.NI, self.SL, self.HB, self.NQ
        xb3 = self.v3(self.xb)
        ptc = [self.pA, self.pB, self.pb2, self.pb3]
        ocols = min(SL, DK)
        noc = ocols // 128
        for og in range(max(1, DK // SL)):
            bufs = []
            for h in range(NQ):
                bufs.append(self.slab_load(self.whalf_src(srcT, og, h, ocols),
                                           cast=True, ocols=ocols,
                                           cache_key=(key, og, h), m0=(m == 0)))
            self.wait("tensor", "sV")
            self.wait("tensor", "sS")
            if m == 0:
                self.emit_slab_wsum(bufs, key, og, ocols)
            for h in range(NQ):
                buf, tgt, slot = bufs[h]
                self.wait("tensor", tgt[0], tgt[1])
                b3 = buf[:, 0:HB * ocols].rearrange("p (i t) -> p i t", t=ocols)
                for i in range(HB):
                    for oc in range(noc):
                        self.em("tensor", lambda e, i=i, oc=oc, b3=b3, h=h, xb3=xb3:
                                e.matmul(ptc[oc][:, 0:TT],
                                         b3[:, i, oc * 128:(oc + 1) * 128],
                                         xb3[:, h * HB + i, 0:TT],
                                         start=(h == 0 and i == 0), stop=False))
            for oc in range(noc):
                self.em("tensor", lambda e, og=og, oc=oc, key=key, ocols=ocols: e.matmul(
                    ptc[oc][:, 0:TT],
                    self.wsum[key][0:1, og * ocols + oc * 128:og * ocols + (oc + 1) * 128],
                    self.nmu1_r, start=False, stop=True))
            self.inc("tensor", "sT")
            self.wait("vector", "sT")
            for oc in range(noc):
                hp = og * noc + oc
                self.em("vector", lambda e, hp=hp, oc=oc, dest=dest: e.tensor_mul(
                    dest[:, hp * TT:(hp + 1) * TT], ptc[oc][:, 0:TT], self.r1b[:, 0:TT]))
            self.inc("vector", "sV")
            for h in range(NQ):
                self.slab_done(bufs[h][2])
        self.wait("tensor", "sV")

    def emit_slab_wsum(self, bufs, key, og, ocols):
        HB, NQ = self.HB, self.NQ
        for h in range(NQ):
            buf, tgt, slot = bufs[h]
            self.wait("tensor", tgt[0], tgt[1])
            b3 = buf[:, 0:HB * ocols].rearrange("p (i t) -> p i t", t=ocols)
            for i in range(HB):
                self.em("tensor", lambda e, i=i, b3=b3, h=h, ocols=ocols: e.matmul(
                    self.pst[0:1, 0:ocols], self.ones_c[:], b3[:, i, :],
                    start=(h == 0 and i == 0), stop=(h == NQ - 1 and i == HB - 1)))
        self.inc("tensor", "sT")
        self.wait("vector", "sT")
        self.em("vector", lambda e, og=og, key=key, ocols=ocols: e.tensor_copy(
            self.wsum[key][0:1, og * ocols:(og + 1) * ocols], self.pst[0:1, 0:ocols]))
        self.inc("vector", "sV")
        self.wait("tensor", "sV")

    def emit_vg_proj(self, m, key, srcT, is_g):
        D, TT, NI, SL, NCH, HB, NQ = (self.D, self.TT, self.NI, self.SL, self.NCH,
                                      self.HB, self.NQ)
        ptc = [self.pb2, self.pb3, self.pb4, self.pb5][:NCH]
        self.wait("vector", "sMP2")
        self.wait("scalar", "sMP2")
        if is_g and m == 0:
            self.wait("scalar", "sV", self.phase2_sg_guard)
            self.wait("scalar", "sFW")
        xb3 = self.v3(self.xb)
        for og in range(D // SL):
            bufs = []
            for h in range(NQ):
                bufs.append(self.slab_load(self.whalf_src(srcT, og, h, SL), cast=True,
                                           cache_key=(key, og, h), m0=(m == 0)))
            if m == 0:
                self.wait("tensor", "sV")
                self.emit_slab_wsum(bufs, key, og, SL)
            self.wait("tensor", "sV")
            self.wait("tensor", "sS")
            for h in range(NQ):
                buf, tgt, slot = bufs[h]
                self.wait("tensor", tgt[0], tgt[1])
                b3 = buf[:].rearrange("p (i t) -> p i t", t=SL)
                for i in range(HB):
                    for tc in range(NCH):
                        self.em("tensor", lambda e, i=i, tc=tc, b3=b3, h=h, xb3=xb3:
                                e.matmul(ptc[tc][:, 0:SL],
                                         xb3[:, h * HB + i, tc * 128:(tc + 1) * 128],
                                         b3[:, i, :], start=(h == 0 and i == 0),
                                         stop=False))
            c0 = m * TT
            for tc in range(NCH):
                self.em("tensor", lambda e, tc=tc, og=og, key=key: e.matmul(
                    ptc[tc][:, 0:SL], self.nmu1_r[0:1, tc * 128:(tc + 1) * 128],
                    self.wsum[key][0:1, og * SL:(og + 1) * SL],
                    start=False, stop=not is_g))
                if is_g:
                    self.em("tensor", lambda e, tc=tc, og=og: e.matmul(
                        ptc[tc][:, 0:SL], self.r1inv_r[0:1, tc * 128:(tc + 1) * 128],
                        self.bg_sb[0:1, og * SL:(og + 1) * SL], start=False, stop=True))
            self.inc("tensor", "sT")
            if is_g:
                self.wait("scalar", "sT")
                for tc in range(NCH):
                    self.em("scalar", lambda e, tc=tc, og=og, m=m: e.activation(
                        self.sg[:, tc * D + og * SL:tc * D + (og + 1) * SL],
                        ptc[tc][:, 0:SL], AF.Silu,
                        scale=self.r1col[:, m * self.NCH + tc:m * self.NCH + tc + 1]))
                self.inc("scalar", "sS")
            else:
                self.wait("vector", "sT")
                for tc in range(NCH):
                    self.em("vector", lambda e, tc=tc, og=og, m=m: e.tensor_scalar(
                        self.vtm[:, tc * D + og * SL:tc * D + (og + 1) * SL],
                        ptc[tc][:, 0:SL],
                        self.r1col[:, m * self.NCH + tc:m * self.NCH + tc + 1], None,
                        op0=ALU.mult))
                self.inc("vector", "sV")
            for h in range(NQ):
                self.slab_done(bufs[h][2])
        if is_g:
            self.wait("tensor", "sS")
        else:
            self.wait("tensor", "sV")

    def emit_chunk_tables(self, m, c, need_eB):
        DK, HP = self.DK, self.HP
        self.wait("tensor", "sS", self.pAB_s)
        self.wait("tensor", "sV", self.pAB_v)
        nh = max(1, HP // 2)
        for hp in range(HP):
            pp = self.pA if hp < nh else self.pB
            off = (hp % nh) * 128
            self.em("tensor", lambda e, hp=hp, c=c, pp=pp, off=off: e.matmul(
                pp[:, off:off + 128],
                self.gk[:, c * DK + hp * 128:c * DK + (hp + 1) * 128],
                self.ut[:], start=True, stop=True))
        self.inc("tensor", "sT")
        self.wait("scalar", "sT")
        half = nh * 128
        self.em("scalar", lambda e, half=half: e.activation(
            self.emB[:, 0:half], self.pA[:, 0:half], AF.Exp, scale=NINV,
            bias=self.lnscc[:, 0:1]))
        if HP > nh:
            self.em("scalar", lambda e, half=half: e.activation(
                self.emB[:, half:2 * half], self.pB[:, 0:half], AF.Exp, scale=NINV,
                bias=self.lnscc[:, 0:1]))
        if need_eB:
            self.em("scalar", lambda e, half=half: e.activation(
                self.eB[:, 0:half], self.pA[:, 0:half], AF.Exp, scale=-NINV))
            if HP > nh:
                self.em("scalar", lambda e, half=half: e.activation(
                    self.eB[:, half:2 * half], self.pB[:, 0:half], AF.Exp, scale=-NINV))
        pa3 = self.pA[:, 0:half].rearrange("p (h t) -> p h t", t=128)
        self.em("scalar", lambda e, pa3=pa3, nh=nh: e.activation(
            self.eA[:, 0:nh].rearrange("p (h x) -> p h x", x=1),
            pa3[:, :, 127:128], AF.Exp, scale=-NINV))
        if HP > nh:
            pb3_ = self.pB[:, 0:half].rearrange("p (h t) -> p h t", t=128)
            self.em("scalar", lambda e, pb3_=pb3_, nh=nh: e.activation(
                self.eA[:, nh:2 * nh].rearrange("p (h x) -> p h x", x=1),
                pb3_[:, :, 127:128], AF.Exp, scale=-NINV))
        self.inc("scalar", "sS")
        self.pAB_s = self.cnt["sS"]

    def emit_ktilde(self, m, c):
        HP, TT = self.HP, self.TT
        self.wait("vector", "sS")
        for hp in range(HP):
            self.em("vector", lambda e, hp=hp, c=c: e.tensor_mul(
                self.k_sb[:, hp * TT + c * 128:hp * TT + (c + 1) * 128],
                self.k_sb[:, hp * TT + c * 128:hp * TT + (c + 1) * 128],
                self.emB[:, hp * 128:(hp + 1) * 128]))
        self.inc("vector", "sV")

    def emit_ktm(self, m, c, fence=True):
        HP, TT = self.HP, self.TT
        self.wait("tensor", "sV")
        for hp in range(HP):
            self.em("tensor", lambda e, hp=hp, c=c: e.transpose(
                self.ptr[:, (hp % 8) * 128:(hp % 8 + 1) * 128],
                self.k_sb[:, hp * TT + c * 128:hp * TT + (c + 1) * 128], self.idn[:]))
        self.inc("tensor", "sT")
        self.wait("scalar", "sT")
        for hp in range(HP):
            self.em("scalar", lambda e, hp=hp: e.activation(
                self.ktm[:, hp * 128:(hp + 1) * 128],
                self.ptr[:, (hp % 8) * 128:(hp % 8 + 1) * 128], AF.Copy))
        self.inc("scalar", "sS")
        self.ktm_s = self.cnt["sS"]
        if fence:
            self.wait("tensor", "sS")

    def emit_sup_wave(self, c, w):
        D, HP = self.D, self.HP
        hps = [hp for hp in range(4 * w, min(HP, 4 * w + 4))]
        pban = [self.pb2, self.pb3]
        for j, hp in enumerate(hps):
            pb = pban[j // 2]
            off = (j % 2) * 256
            self.em("tensor", lambda e, hp=hp, c=c, pb=pb, off=off: e.matmul(
                pb[:, off:off + 256], self.ktm[:, hp * 128:(hp + 1) * 128],
                self.vtm[:, c * D + hp * 256:c * D + (hp + 1) * 256],
                start=True, stop=True))
        self.inc("tensor", "sT")
        self.wait("vector", "sT")
        for j, hp in enumerate(hps):
            pb = pban[j // 2]
            off = (j % 2) * 256
            self.em("vector", lambda e, hp=hp, pb=pb, off=off: e.tensor_add(
                self.S[0:64, hp * 128:(hp + 1) * 128],
                self.S[0:64, hp * 128:(hp + 1) * 128], pb[0:64, off:off + 128]))
            self.em("vector", lambda e, hp=hp, pb=pb, off=off: e.tensor_add(
                self.S[64:128, hp * 128:(hp + 1) * 128],
                self.S[64:128, hp * 128:(hp + 1) * 128],
                pb[64:128, off + 128:off + 256]))
        for hp in hps:
            self.em("vector", lambda e, hp=hp: e.tensor_scalar(
                self.S[:, hp * 128:(hp + 1) * 128], self.S[:, hp * 128:(hp + 1) * 128],
                self.eA[:, hp:hp + 1], None, op0=ALU.mult))
        self.inc("vector", "sV")
        self.wait("tensor", "sV")

    def emit_attn_chunk(self, m, c):
        HP, TT, D, H = self.HP, self.TT, self.D, self.H
        self.wait("vector", "sCW")
        self.wait("vector", "sS")
        for hp in range(HP):
            self.em("vector", lambda e, hp=hp, c=c: e.tensor_mul(
                self.qt_sb[:, hp * 128:(hp + 1) * 128],
                self.q_sb[:, hp * TT + c * 128:hp * TT + (c + 1) * 128],
                self.eB[:, hp * 128:(hp + 1) * 128]))
        v_qt = self.inc("vector", "sV")
        self.wait("tensor", "sV", v_qt)
        nw = (HP + 3) // 4
        for w in range(nw):
            hps = [hp for hp in range(4 * w, min(HP, 4 * w + 4))]
            for j, hp in enumerate(hps):
                self.em("tensor", lambda e, hp=hp, c=c, j=j: e.matmul(
                    self.pb2[:, j * 128:(j + 1) * 128],
                    self.k_sb[0:64, hp * TT + c * 128:hp * TT + (c + 1) * 128],
                    self.qt_sb[0:64, hp * 128:(hp + 1) * 128], start=True, stop=True))
                self.em("tensor", lambda e, hp=hp, c=c, j=j: e.matmul(
                    self.pb3[:, j * 128:(j + 1) * 128],
                    self.k_sb[64:128, hp * TT + c * 128:hp * TT + (c + 1) * 128],
                    self.qt_sb[64:128, hp * 128:(hp + 1) * 128], start=True, stop=True))
            self.inc("tensor", "sT")
            self.wait("vector", "sT")
            wd = 128 * len(hps)
            self.em("vector", lambda e, wd=wd: e.tensor_mul(
                self.ptm[:, 0:wd], self.pb2[:, 0:wd], self.trimask4[:, 0:wd]))
            self.em("vector", lambda e, wd=wd: e.tensor_mul(
                self.ptm[:, 512:512 + wd], self.pb3[:, 0:wd], self.trimask4[:, 0:wd]))
            self.inc("vector", "sV")
            self.wait("tensor", "sV")
            pban = [self.pb4, self.pb5]
            for j, hp in enumerate(hps):
                pb = pban[j // 2]
                off = (j % 2) * 256
                self.em("tensor", lambda e, hp=hp, c=c, j=j, pb=pb, off=off: e.matmul(
                    pb[:, off:off + 128], self.ptm[:, j * 128:(j + 1) * 128],
                    self.vtm[:, c * D + hp * 256:c * D + hp * 256 + 128],
                    start=True, stop=False))
                self.em("tensor", lambda e, hp=hp, pb=pb, off=off: e.matmul(
                    pb[:, off:off + 128], self.qt_sb[0:64, hp * 128:(hp + 1) * 128],
                    self.S16[0:64, hp * 128:(hp + 1) * 128], start=False, stop=True))
                self.em("tensor", lambda e, hp=hp, c=c, j=j, pb=pb, off=off: e.matmul(
                    pb[:, off + 128:off + 256],
                    self.ptm[:, 512 + j * 128:512 + (j + 1) * 128],
                    self.vtm[:, c * D + hp * 256 + 128:c * D + (hp + 1) * 256],
                    start=True, stop=False))
                self.em("tensor", lambda e, hp=hp, pb=pb, off=off: e.matmul(
                    pb[:, off + 128:off + 256],
                    self.qt_sb[64:128, hp * 128:(hp + 1) * 128],
                    self.S16[64:128, hp * 128:(hp + 1) * 128], start=False, stop=True))
            self.inc("tensor", "sT")
            t_o = self.cnt["sT"]
            self.wait("tensor", "sS", self.ktm_s)
            pban2 = [self.pb2, self.pb3]
            for j, hp in enumerate(hps):
                pb = pban2[j // 2]
                off = (j % 2) * 256
                self.em("tensor", lambda e, hp=hp, c=c, pb=pb, off=off: e.matmul(
                    pb[:, off:off + 256], self.ktm[:, hp * 128:(hp + 1) * 128],
                    self.vtm[:, c * D + hp * 256:c * D + (hp + 1) * 256],
                    start=True, stop=True))
            self.inc("tensor", "sT")
            self.wait("scalar", "sT", t_o)
            h0 = hps[0]
            self.em("scalar", lambda e, h0=h0: e.activation(
                self.otm[:, h0 * 256:h0 * 256 + 512], self.pb4[:, 0:512], AF.Copy))
            if len(hps) > 2:
                self.em("scalar", lambda e, h0=h0: e.activation(
                    self.otm[:, h0 * 256 + 512:h0 * 256 + 1024], self.pb5[:, 0:512],
                    AF.Copy))
            self.inc("scalar", "sS")
            self.wait("vector", "sT")
            for j, hp in enumerate(hps):
                pb = pban2[j // 2]
                off = (j % 2) * 256
                self.em("vector", lambda e, hp=hp, pb=pb, off=off: e.tensor_add(
                    self.S[0:64, hp * 128:(hp + 1) * 128],
                    self.S[0:64, hp * 128:(hp + 1) * 128], pb[0:64, off:off + 128]))
                self.em("vector", lambda e, hp=hp, pb=pb, off=off: e.tensor_add(
                    self.S[64:128, hp * 128:(hp + 1) * 128],
                    self.S[64:128, hp * 128:(hp + 1) * 128],
                    pb[64:128, off + 128:off + 256]))
            for hp in hps:
                self.em("vector", lambda e, hp=hp: e.tensor_scalar(
                    self.S[:, hp * 128:(hp + 1) * 128],
                    self.S[:, hp * 128:(hp + 1) * 128],
                    self.eA[:, hp:hp + 1], None, op0=ALU.mult))
            self.inc("vector", "sV")
            wdd = 128 * len(hps)
            self.wait("scalar", "sV")
            self.em("scalar", lambda e, h0=h0, wdd=wdd: e.activation(
                self.S16[:, h0 * 128:h0 * 128 + wdd],
                self.S[:, h0 * 128:h0 * 128 + wdd], AF.Copy))
            self.inc("scalar", "sS")
            self.wait("vector", "sS")
            for hh in range(2 * h0, 2 * h0 + 2 * len(hps)):
                self.em("vector", lambda e, hh=hh: e.bn_stats(
                    self.bns[:, hh * 6:(hh + 1) * 6],
                    self.otm[:, hh * 128:(hh + 1) * 128]))
            self.inc("vector", "sV")
            self.wait("tensor", "sV")
            self.wait("tensor", "sS")
        m1 = self.bns[:, 1:H * 6:6]
        m2 = self.bns[:, 4:H * 6:6]
        cv1 = self.bns[:, 2:H * 6:6]
        cv2 = self.bns[:, 5:H * 6:6]
        hmu = self.bagg[:, 0:H]
        hva = self.bagg[:, H:2 * H]
        self.em("vector", lambda e, hmu=hmu, m1=m1, m2=m2: e.tensor_add(hmu, m1, m2))
        self.em("vector", lambda e, hva=hva, cv1=cv1, cv2=cv2: e.tensor_add(hva, cv1,
                                                                            cv2))
        self.em("vector", lambda e, m1=m1, m2=m2: e.tensor_sub(self.hr[:, 0:H], m1, m2))
        self.inc("vector", "sV")
        self.wait("vector", "sV")
        self.em("vector", lambda e, hmu=hmu: e.tensor_scalar(hmu, hmu, 0.5, None,
                                                             op0=ALU.mult))
        self.em("vector", lambda e, hva=hva: e.tensor_scalar(hva, hva, 1.0 / 128.0,
                                                             None, op0=ALU.mult))
        self.em("vector", lambda e: e.tensor_mul(self.hr[:, 0:H], self.hr[:, 0:H],
                                                 self.hr[:, 0:H]))
        self.inc("vector", "sV")
        self.wait("vector", "sV")
        self.em("vector", lambda e, hva=hva: e.scalar_tensor_tensor(
            hva, self.hr[:, 0:H], 0.25, hva, op0=ALU.mult, op1=ALU.add))
        self.inc("vector", "sV")
        self.wait("scalar", "sV")
        self.em("scalar", lambda e, hva=hva: e.activation(self.hr[:, 0:H], hva, AF.Ln,
                                                          bias=self.epsc[:, 0:1]))
        self.inc("scalar", "sS")
        self.wait("scalar", "sS")
        self.em("scalar", lambda e: e.activation(self.hr[:, 0:H], self.hr[:, 0:H],
                                                 AF.Exp, scale=-0.5))
        self.inc("scalar", "sS")
        self.wait("vector", "sS")
        for h in range(H):
            self.em("vector", lambda e, h=h, hmu=hmu: e.tensor_scalar(
                self.otm[:, h * 128:(h + 1) * 128], self.otm[:, h * 128:(h + 1) * 128],
                hmu[:, h:h + 1], self.hr[:, h:h + 1], op0=ALU.subtract, op1=ALU.mult))
        self.inc("vector", "sV")
        self.wait("vector", "sV")
        for h in range(H):
            self.em("vector", lambda e, h=h, c=c: e.tensor_mul(
                self.otm[:, h * 128:(h + 1) * 128], self.otm[:, h * 128:(h + 1) * 128],
                self.sg[:, c * D + h * 128:c * D + (h + 1) * 128]))
        self.inc("vector", "sV")
        self.wait("tensor", "sV")
        for w0 in range(0, H, 8):
            for h in range(w0, min(H, w0 + 8)):
                self.em("tensor", lambda e, h=h: e.transpose(
                    self.ptr[:, (h % 8) * 128:(h % 8 + 1) * 128],
                    self.otm[:, h * 128:(h + 1) * 128], self.idn[:]))
            self.inc("tensor", "sT")
            self.wait("scalar", "sT")
            for h in range(w0, min(H, w0 + 8)):
                self.em("scalar", lambda e, h=h, c=c: e.activation(
                    self.yfm[:, h * TT + c * 128:h * TT + (c + 1) * 128],
                    self.ptr[:, (h % 8) * 128:(h % 8 + 1) * 128], AF.Copy))
            self.inc("scalar", "sS")
            self.wait("tensor", "sS")

    def emit_out_proj(self, m, is_ffn):
        D, TT, NI, SL, HB, NQ = self.D, self.TT, self.NI, self.SL, self.HB, self.NQ
        src3 = self.v3(self.yfm)
        ptc = [self.pA, self.pB, self.pb2, self.pb3]
        if is_ffn:
            self.wait("sync", "sFW", self.wtsc_done)
        elif m == 0:
            self.wait("vector", "sFW")
        for og in range(D // SL):
            bufs = []
            for h in range(NQ):
                if is_ffn:
                    bufs.append(self.slab_load(
                        self.wtsc[og].rearrange("p (i t) -> p i t", t=SL)
                        [:, h * HB:(h + 1) * HB, :], cast=False))
                else:
                    bufs.append(self.slab_load(self.whalf_src(self.WoT, og, h, SL),
                                               cast=True, cache_key=("o", og, h),
                                               m0=(m == 0)))
            self.wait("tensor", "sV")
            self.wait("tensor", "sS")
            for h in range(NQ):
                buf, tgt, slot = bufs[h]
                self.wait("tensor", tgt[0], tgt[1])
                b3 = buf[:].rearrange("p (i t) -> p i t", t=SL)
                for i in range(HB):
                    for oc in range(4):
                        last = (not is_ffn) and h == NQ - 1 and i == HB - 1
                        self.em("tensor", lambda e, i=i, oc=oc, b3=b3, h=h, src3=src3,
                                last=last: e.matmul(
                                    ptc[oc][:, 0:TT], b3[:, i, oc * 128:(oc + 1) * 128],
                                    src3[:, h * HB + i, 0:TT],
                                    start=(h == 0 and i == 0), stop=last))
            if is_ffn:
                for oc in range(4):
                    self.em("tensor", lambda e, og=og, oc=oc: e.matmul(
                        ptc[oc][:, 0:TT],
                        self.wsum["t"][0:1, og * SL + oc * 128:og * SL + (oc + 1) * 128],
                        self.nmu3_r, start=False, stop=False))
                    self.em("tensor", lambda e, og=og, oc=oc: e.matmul(
                        ptc[oc][:, 0:TT],
                        self.bf_sb[0:1, og * SL + oc * 128:og * SL + (oc + 1) * 128],
                        self.r3is_r, start=False, stop=True))
            self.inc("tensor", "sT")
            if is_ffn:
                self.wait("vector", "sT")
                for oc in range(4):
                    ic = og * 4 + oc
                    self.em("vector", lambda e, ic=ic, oc=oc: e.tensor_mul(
                        self.ao[:, ic * TT:(ic + 1) * TT], ptc[oc][:, 0:TT],
                        self.r3b[:, 0:TT]))
                    self.sq_acc(self.ao[:, ic * TT:(ic + 1) * TT], ic,
                                wv=self.cnt["sVo"])
                self.inc("vector", "sV")
            else:
                self.wait("scalar", "sT")
                for oc in range(4):
                    ic = og * 4 + oc
                    self.em("scalar", lambda e, ic=ic, oc=oc: e.activation(
                        self.ao[:, ic * TT:(ic + 1) * TT], ptc[oc][:, 0:TT],
                        AF.Copy))
                    self.sq_acc(ptc[oc][:, 0:TT], ic)
                self.inc("vector", "sV")
                self.wait("tensor", "sSo", self.cnt["sSo"])
            for h in range(NQ):
                self.slab_done(bufs[h][2])
        self.wait("tensor", "sV")
        self.wait("tensor", "sSo", self.cnt["sSo"])

    def emit_post_ln(self, m, cast_xb):
        self.emit_post_ln_head(m, cast_xb)
        self.emit_post_ln_tail(m, cast_xb)

    def emit_post_ln_head(self, m, cast_xb):
        D, TT, NI = self.D, self.TT, self.NI
        a3 = self.v3(self.ao)
        self.feat_stats(a3, inline_sq=False)
        self.stats_to_mu_var()
        self.wait("scalar", "sV")
        self.em("scalar", lambda e: e.activation(self.e2_r, self.e2_r, AF.Ln,
                                                 bias=self.epsc[0:1, 0:1]))
        self.em("scalar", lambda e: e.activation(self.rA_r, self.e2_r, AF.Exp,
                                                 scale=-0.5))
        self.inc("scalar", "sS")
        self.wait("vector", "sS")
        if cast_xb:
            self.em("vector", lambda e: e.tensor_mul(self.murA_r, self.mu_r,
                                                     self.rA_r))
        else:
            self.em("vector", lambda e: e.tensor_mul(self.mu_r, self.mu_r,
                                                     self.rA_r))
            self.em("vector", lambda e: e.tensor_add(self.mu_r, self.mu_r,
                                                     self.murA_r))
        self.inc("vector", "sV")
        self.wait("tensor", "sS")
        self.bcast_f32(self.rA_r, self.prb[:, 0:TT])
        if not cast_xb:
            self.wait("tensor", "sV")
            self.bcast_f32(self.mu_r, self.pmb[:, 0:TT])

    def emit_post_ln_tail(self, m, cast_xb):
        TT, NI = self.TT, self.NI
        a3 = self.v3(self.ao)
        xf3 = self.v3(self.xf)
        yf3 = self.v3(self.yfm)
        self.wait("scalar", "sT")
        for i in range(NI):
            self.em("vector", lambda e, i=i, a3=a3: e.tensor_mul(
                self.yrbb[:, 0:TT], a3[:, i, 0:TT], self.prb[:, 0:TT]))
            self.em("vector", lambda e, i=i, xf3=xf3: e.tensor_add(
                xf3[:, i, 0:TT], xf3[:, i, 0:TT], self.yrbb[:, 0:TT]))
            if cast_xb:
                cv = self.cnt["sVo"]
                self.wait("scalar", "sVo", cv)
                self.em("scalar", lambda e, i=i, xf3=xf3, yf3=yf3: e.activation(
                    yf3[:, i, 0:TT], xf3[:, i, 0:TT], AF.Copy))
                self.sq_acc(xf3[:, i, 0:TT], i, wv=cv)
            else:
                self.em("vector", lambda e, i=i, xf3=xf3: e.tensor_sub(
                    xf3[:, i, 0:TT], xf3[:, i, 0:TT], self.pmb[:, 0:TT]))
        self.inc("vector", "sV")

    def emit_ln3(self, m):
        D, TT = self.D, self.TT
        self.feat_stats(self.v3(self.yfm), inline_sq=False)
        self.stats_to_mu_var()
        self.wait("vector", "sT")
        self.em("vector", lambda e: e.tensor_scalar(self.nmu3_r, self.pst[0:1, 0:TT],
                                                    -1.0 / D, None, op0=ALU.mult))
        self.inc("vector", "sV")
        self.wait("scalar", "sV")
        self.em("scalar", lambda e: e.activation(self.e2_r, self.e2_r, AF.Ln,
                                                 bias=self.epsc[0:1, 0:1]))
        self.inc("scalar", "sS")
        self.wait("scalar", "sS")
        self.em("scalar", lambda e: e.activation(self.rA_r, self.e2_r, AF.Exp,
                                                 scale=-0.5, bias=self.lnS_p[0:1, 0:1]))
        self.em("scalar", lambda e: e.activation(self.r3is_r, self.e2_r, AF.Exp,
                                                 scale=0.5, bias=self.lnS_n[0:1, 0:1]))
        self.inc("scalar", "sS")
        self.wait("tensor", "sS")
        self.bcast_f32(self.rA_r, self.r3b[:, 0:TT])

    def emit_phase1_macro(self, m, guards):
        TT, NM = self.TT, self.NM
        if m == 0:
            xbt = self.dma("gpsimd", self.v3(self.xb),
                           self.xT[:, 0:TT].rearrange("(i p) t -> p i t", p=128),
                           "sXDg")
            self.wait("tensor", "sXDg", xbt)
            self.wait("vector", "sXDg", xbt)
            self.wait("scalar", "sXDg", xbt)
            self.emit_ln1(0)
        self.emit_kg(m)
        if guards.get("k_store"):
            self.wait("vector", "sKW", guards["k_store"])
        self.emit_qk_proj(m, "k", self.WkT, self.k_sb[:])
        if guards.get("v_store"):
            self.wait("vector", "sVW", guards["v_store"])
        self.emit_vg_proj(m, "v", self.WvT, is_g=False)
        guards["xb_free"] = (self.cnt["sT"], self.cnt["sV"])
        xbt = None
        if m + 1 < NM:
            self.wait("gpsimd", "sT", guards["xb_free"][0])
            self.wait("gpsimd", "sV", guards["xb_free"][1])
            xbt = self.dma("gpsimd", self.v3(self.xb),
                           self.xT[:, (m + 1) * TT:(m + 2) * TT]
                           .rearrange("(i p) t -> p i t", p=128), "sXDg")
        self.pAB_s = self.cnt["sS"]
        self.pAB_v = self.cnt["sV"]
        for c in range(self.NCH):
            self.emit_chunk_tables(m, c, need_eB=False)
            self.emit_ktilde(m, c)
            self.emit_ktm(m, c)
            self.wait("tensor", "sV")
            for w in range((self.HP + 3) // 4):
                self.emit_sup_wave(c, w)
        if m + 1 < NM:
            self.wait("tensor", "sXDg", xbt)
            self.wait("vector", "sXDg", xbt)
            self.wait("scalar", "sXDg", xbt)
            self.emit_ln1(m + 1)
        self.wait("sync", "sV")
        guards["k_store"] = self.dma("sync", self.ksc[m], self.k_sb[:], "sKW")
        guards["v_store"] = self.dma("sync", self.vsc[m], self.vtm[:], "sVW")

    def emit_state_handoff(self):
        HP = self.HP
        self.wait("sync", "sV")
        st = self.dma("sync", self.s_loc, self.S[:], "sMP1")
        self.wait("gpsimd", "sMP1", st)
        rg = [list(range(self.ncores))]
        self.em("gpsimd", lambda e, rg=rg: e.collective_compute(
            "AllGather", ALU.bypass, rg, [self.s_loc], [self.s_all]))
        self.inc("gpsimd", "sG")
        self.wait("sync", "sG")
        self.wait("vector", "sFLG")
        for r in range(self.ncores):
            tgt = self.dma("sync", self.Sbr_ap, self.s_all[r * 128:(r + 1) * 128, :],
                           "sPL")
            self.wait("vector", "sPL", tgt)
            if r == 0:
                self.em("vector", lambda e, r=r: e.tensor_scalar(
                    self.S[:], self.Sbr_ap, self.selc[:, r:r + 1], None, op0=ALU.mult))
            else:
                self.em("vector", lambda e, r=r: e.scalar_tensor_tensor(
                    self.S[:], self.Sbr_ap, self.selc[:, r:r + 1], self.S[:],
                    op0=ALU.mult, op1=ALU.add))
            self.inc("vector", "sV")
            if r < self.ncores - 1:
                self.wait("sync", "sV")
        self.wait("vector", "sV")
        self.em("vector", lambda e: e.tensor_copy(self.S16[:], self.S[:]))
        self.inc("vector", "sV")

    def emit_p2_proj(self, m, xbt):
        TT = self.TT
        self.wait("sync", "sT")
        self.wait("sync", "sPR")
        self.wait("sync", "sMP1")
        self.wait("sync", "sV")
        self.wait("sync", "sS")
        self.dma("sync", self.rA_r, self.scrd[m, 0:TT], "sPL")
        self.dma("sync", self.nmu1_r, self.scrb[m, 0, 0:TT], "sPL")
        pr3 = self.dma("sync", self.r1inv_r, self.scrb[m, 1, 0:TT], "sPL")
        self.wait("tensor", "sPL", pr3)
        self.wait("vector", "sPL", pr3)
        self.wait("scalar", "sPL", pr3)
        self.wait("tensor", "sXDg", xbt)
        self.wait("vector", "sXDg", xbt)
        self.emit_r1b()
        self.emit_qk_proj(m, "q", self.WqT, self.q_sb[:])
        self.emit_vg_proj(m, "g", self.WgT, is_g=True)
        self.emit_kg(m)
        self.p2_xb_rd = (self.cnt["sT"], self.cnt["sV"], self.cnt["sS"])

    def emit_phase2_macro(self, m, guards):
        TT, NM, NCH = self.TT, self.NM, self.NCH
        if guards.get("out_store"):
            self.wait("sync", "sOUT", guards["out_store"])
        if m == 0:
            self.wait("sync", "sS", self.phase2_xf_guard[0])
            self.wait("sync", "sV", self.phase2_xf_guard[1])
        xft = self.dma("sync", self.v3(self.xf),
                       self.xT[:, m * TT:(m + 1) * TT].rearrange("(i p) t -> p i t",
                                                                 p=128), "sXF")
        xbt = None
        if m + 1 < NM:
            self.wait("gpsimd", "sT", self.p2_xb_rd[0])
            self.wait("gpsimd", "sV", self.p2_xb_rd[1])
            self.wait("gpsimd", "sS", self.p2_xb_rd[2])
            xbt = self.dma("gpsimd", self.v3(self.xb),
                           self.xT[:, (m + 1) * TT:(m + 2) * TT]
                           .rearrange("(i p) t -> p i t", p=128), "sXDg")
        if guards.get("kv_free"):
            self.wait("sync", "sT", guards["kv_free"][0])
            self.wait("sync", "sV", guards["kv_free"][1])
        self.wait("sync", "sKW")
        self.wait("sync", "sVW")
        kst = self.dma("sync", self.k_sb[:], self.ksc[m], "sKS")
        vst = self.dma("sync", self.vtm[:], self.vsc[m], "sVS")
        self.wait("tensor", "sKS", kst)
        self.wait("vector", "sKS", kst)
        self.wait("tensor", "sVS", vst)
        self.wait("vector", "sVS", vst)
        self.pAB_s = self.cnt["sS"]
        self.pAB_v = self.cnt["sV"]
        for c in range(NCH):
            self.emit_chunk_tables(m, c, need_eB=True)
            self.emit_ktm(m, c, fence=False)
            self.emit_attn_chunk(m, c)
        self.slab_free[3] = dict(sT=self.cnt["sT"], sV=self.cnt["sV"], sS=0,
                                 sCW=self.slab_free[3]["sCW"])
        self.emit_out_proj(m, is_ffn=False)
        self.wait("vector", "sXF", xft)
        self.emit_post_ln_head(m, cast_xb=True)
        if m + 1 < NM:
            self.emit_p2_proj(m + 1, xbt)
        self.emit_post_ln_tail(m, cast_xb=True)
        self.emit_ln3(m)
        self.emit_out_proj(m, is_ffn=True)
        self.emit_post_ln(m, cast_xb=False)
        self.wait("sync", "sV")
        guards["out_store"] = self.dma(
            "sync",
            self.outT[:, m * TT:(m + 1) * TT].rearrange("(i p) t -> p i t", p=128),
            self.v3(self.xf), "sOUT")
        guards["kv_free"] = (self.cnt["sT"], self.cnt["sV"])

    def emit_all(self):
        self.emit_setup()
        guards = {}
        NM = self.NM
        nslab_f = self.D // self.SL
        for m in range(NM):
            if m < nslab_f:
                self.emit_ffn_pass1(m)
            if m == nslab_f:
                self.emit_ffn_scale()
            self.emit_phase1_macro(m, guards)
            og = m - (NM - nslab_f)
            if og >= 0:
                self.emit_ffn_pass2(og)
        self.wtsc_done = self.cnt["sFW"]
        self.phase2_sg_guard = self.cnt["sV"]
        self.wait("gpsimd", "sT", guards["xb_free"][0])
        self.wait("gpsimd", "sV", guards["xb_free"][1])
        xbt0 = self.dma("gpsimd", self.v3(self.xb),
                        self.xT[:, 0:self.TT].rearrange("(i p) t -> p i t", p=128),
                        "sXDg")
        self.emit_p2_proj(0, xbt0)
        self.emit_state_handoff()
        self.phase2_xf_guard = (self.cnt["sS"], self.cnt["sV"])
        guards = {"kv_free": (self.cnt["sT"], self.cnt["sV"]),
                  "out_store": 0}
        for m in range(NM):
            self.emit_phase2_macro(m, guards)
        self.wait("sync", "sOUT")
        self.wait("sync", "sKW")
        self.wait("sync", "sVW")
        self.wait("sync", "sL")

    def finalize(self):
        blk = self.ctx.enter_context(self.nc.Block())

        def run(e, eng):
            for item in self.prog[eng]:
                if item[0] == "w":
                    e.wait_ge(item[1], item[2])
                elif item[0] == "op":
                    ins = item[1](e)
                    if item[2] is not None:
                        ins.then_inc(self.sem[item[2]], item[3])
        for eng in ("sync", "gpsimd", "vector", "scalar", "tensor"):
            getattr(blk, eng)(lambda e, eng=eng: run(e, eng))
        self.ctx.close()
        return self.nc


def build(D=2048, T=4096, TT=512, RK=16, ncores=8):
    b = GlaBuilder(D=D, T=T, TT=TT, RK=RK, ncores=ncores)
    b.emit_all()
    return b.finalize()


def kernel(**inputs):
    x = np.asarray(inputs["L0_attn_x"], dtype=np.float32)
    Bn, L, D = x.shape
    T = L // 2
    n_cores = 8

    def tr(a):
        return np.ascontiguousarray(np.asarray(a, dtype=np.float32).T)

    wmap = dict(
        WqT=tr(inputs["L0_attn_Wq"]),
        WkT=tr(inputs["L0_attn_Wk"]),
        Wkg1T=tr(inputs["L0_attn_Wkg1"]),
        Wkg2T=tr(inputs["L0_attn_Wkg2"]),
        bkg2=np.asarray(inputs["L0_attn_bkg2"], dtype=np.float32).reshape(1, -1),
        WvT=tr(inputs["L0_attn_Wv"]),
        WgT=tr(inputs["L0_attn_Wg"]),
        bgi=np.asarray(inputs["L0_attn_bg"], dtype=np.float32).reshape(1, -1),
        WoT=tr(inputs["L0_attn_Wo"]),
        WfT=tr(inputs["L0_ffn_weight"]),
        bfi=np.asarray(inputs["L0_ffn_bias"], dtype=np.float32).reshape(1, -1),
    )
    nc = build(D=D, T=T)
    in_maps = []
    for c in range(n_cores):
        b_i, h_i = c // 2, c % 2
        mm = dict(wmap)
        mm["xT"] = np.ascontiguousarray(x[b_i, h_i * T:(h_i + 1) * T, :].T)
        mm["flag"] = np.full((128, 1), float(h_i), dtype=np.float32)
        sel = np.zeros((128, 8), dtype=np.float32)
        if h_i == 1:
            sel[:, c - 1] = 1.0
        mm["selv"] = sel
        in_maps.append(mm)
    res = run_bass_kernel_spmd(nc, in_maps, core_ids=list(range(n_cores)))
    out = np.empty_like(x)
    for c in range(n_cores):
        b_i, h_i = c // 2, c % 2
        out[b_i, h_i * T:(h_i + 1) * T, :] = res.results[c]["outT"].T
    return out
